# version 07 (final)
# baseline (speedup 1.0000x reference)
import numpy as np
import ml_dtypes
import concourse.bass as bass
import concourse.mybir as mybir
from concourse.bass_utils import run_bass_kernel_spmd

F32 = mybir.dt.float32
BF16 = mybir.dt.bfloat16
I32 = mybir.dt.int32
AF = mybir.ActivationFunctionType
ALU = mybir.AluOpType

B, L, D = 2, 8192, 1024
H, DH = 16, 64
EH = 32
NDIMS = 512
FH = 8
ED = 4096
CH = 32
RNG = 256
NRNG = L // RNG
NCH = RNG // CH
EPS_LN = 1e-5
EPS_RMS = 1e-6
MAGIC = 0x51000000
LN8 = float(np.log(8.0))

bf16 = ml_dtypes.bfloat16



def build_inputs_for_core(c, I):
    b, P, s = c // 4, (c % 4) // 2, c % 2
    A = lambda x: np.ascontiguousarray(np.asarray(x, np.float32))
    d = {}
    d["xb"] = A(I["L0_attn_x"][b])
    d["c_col"] = A(np.asarray(I["L0_c"], np.float32)[b].reshape(8, 128).T)
    d["wmod"] = A(I["L0_wmod"])
    d["WkT"] = A(np.asarray(I["L0_attn_Wk"], np.float32).T)
    d["WqT"] = A(np.asarray(I["L0_attn_Wq"], np.float32).T)
    cols = np.array([(4 * q + hl) * DH + EH * s + e
                     for q in range(4) for hl in range(4) for e in range(EH)])
    d["WvT"] = A(np.asarray(I["L0_attn_Wv"], np.float32).T[:, cols])
    d["WgT"] = A(np.asarray(I["L0_attn_Wg"], np.float32).T[:, cols])
    d["WbT"] = A(np.asarray(I["L0_attn_Wb"], np.float32).T[:, cols])
    d["WaT"] = A(np.asarray(I["L0_attn_Wa"], np.float32).T)
    d["dtb_col"] = A(np.asarray(I["L0_attn_dt_bias"], np.float32).reshape(H, 1))
    d["alog_col"] = A(np.asarray(I["L0_attn_A_log"], np.float32).reshape(H, 1))
    nw = np.asarray(I["L0_attn_norm_w"], np.float32)[EH * s: EH * s + EH]
    d["normw_col"] = A(np.tile(nw, 4).reshape(128, 1))
    d["WoT"] = A(np.asarray(I["L0_attn_Wo"], np.float32).T[cols, :])
    d["bo_row"] = A(np.asarray(I["L0_attn_bo"], np.float32).reshape(1, D))
    d["n1w_row"] = A(np.asarray(I["L0_ffn_n1_w"], np.float32).reshape(1, D))
    d["n2w_row"] = A(np.asarray(I["L0_ffn_n2_w"], np.float32).reshape(1, D))
    qkv = np.asarray(I["L0_ffn_qkv_w"], np.float32)
    hcols = np.arange(FH * s * DH, FH * s * DH + NDIMS)
    d["fqT"] = A(qkv[0:D].T[:, hcols])
    d["fkT"] = A(qkv[D:2 * D].T[:, hcols])
    d["fvT"] = A(qkv[2 * D:3 * D].T[:, hcols])
    d["owT"] = A(np.asarray(I["L0_ffn_o_w"], np.float32).T[hcols, :])
    d["gateT"] = A(np.asarray(I["L0_ffn_gate_w"], np.float32).T)
    d["upT"] = A(np.asarray(I["L0_ffn_up_w"], np.float32).T)
    d["downT"] = A(np.asarray(I["L0_ffn_down_w"], np.float32).T)
    d["ident"] = np.eye(128, dtype=np.float32)
    m32 = (np.arange(32)[:, None] <= np.arange(32)[None, :]).astype(np.float32)
    d["incmask"] = A(np.tile(m32, (1, 16)))
    rm = np.ones((H, RNG), np.float32)
    rm[:, 0::CH] = 0.0
    d["resetm"] = A(rm)
    bd = np.zeros((128, 2), np.float32)
    bd[0:64, 0] = 1.0
    bd[64:128, 1] = 1.0
    d["bd64"] = bd
    bd4 = np.zeros((128, 4), np.float32)
    for hl in range(4):
        bd4[32 * hl:32 * hl + 32, hl] = 1.0
    d["bd32"] = bd4
    d["ones_row"] = np.ones((1, 128), np.float32)
    d["iotaQ"] = A(np.broadcast_to(np.arange(512, dtype=np.float32), (128, 512)))
    qoffc = np.zeros((128, 64), np.float32)
    for qi in range(8):
        for mi in range(8):
            qoffc[:, 8 * qi + mi] = (512 * (2 * qi + P) - 128 * (8 * qi + mi)
                                     - np.arange(128))
    d["qoffc"] = A(qoffc)
    d["meta"] = np.array([[s, P, b, MAGIC]], np.int32)
    return d



ENGINES = ("sync", "tensor", "vector", "scalar", "gpsimd")


class KB:
    _rc = [0]

    def __init__(self, nc, g):
        self.nc = nc
        self.g = g
        self.q = {e: [] for e in ENGINES}
        self.sems = {}
        self.cnt = {}
        self._ctx = []

    def sb(self, name, shape, dt=F32):
        cm = self.nc.sbuf_tensor(name, shape, dt)
        t = cm.__enter__()
        self._ctx.append(cm)
        return t

    def init_arena(self, cols):
        self.arena = self.sb("arena", [128, cols])
        self.aoff = 0
        self.atail0 = cols

    def asb(self, name, shape, dt=F32, tail=False):
        import math
        free = 1
        for d in shape[1:]:
            free *= d
        nb = free * mybir.dt.size(dt)
        cols = (nb + 3) // 4
        if tail:
            self.atail0 -= cols
            off = self.atail0
        else:
            off = self.aoff
            self.aoff += cols
        assert self.aoff <= self.atail0, \
            "arena overflow at %s: %d > %d" % (name, self.aoff, self.atail0)
        v = self.arena[:, off:off + cols]
        if dt != F32:
            v = v.bitcast(dt)
        if shape[0] != 128:
            v = v[0:shape[0]]
        if len(shape) > 2:
            pass
        return v

    def areset(self, tailtoo=True):
        self.aoff = 0
        if tailtoo:
            self.atail0 = self.arena.shape[1]

    def ps(self, name, shape, dt=F32):
        cm = self.nc.psum_tensor(name, shape, dt)
        t = cm.__enter__()
        self._ctx.append(cm)
        return t

    def sem(self, name):
        h = self.nc.alloc_semaphore(name)
        self.sems[name] = h
        self.cnt[name] = 0
        return h

    def op(self, eng, meth, *args, inc=None, incn=1, **kw):
        if inc is not None:
            self.cnt[inc] += incn
        sems = self.sems
        rec = {"incs": [] if inc is None else [(inc, incn)], "meth": meth}

        def f(h, meth=meth, args=args, kw=kw, rec=rec):
            ins = getattr(h, meth)(*args, **kw)
            KB._emit_incs(h, ins, rec, sems)
        f.rec = rec
        self.q[eng].append(f)

    @staticmethod
    def _emit_incs(h, ins, rec, sems):
        tot, order = {}, []
        for nm, n in rec["incs"]:
            if nm not in tot:
                tot[nm] = 0
                order.append(nm)
            tot[nm] += n
        assert len(order) <= 1, "instruction needs %d sem updates: %s" % (
            len(order), order)
        for nm in order:
            ins.then_inc(sems[nm], tot[nm])

    def mark(self, eng, inc, incn=1):
        for f in reversed(self.q[eng]):
            rec = getattr(f, "rec", None)
            if rec is not None:
                n = 16 if rec.get("meth") == "dma_start" else incn
                self.cnt[inc] += n
                rec["incs"].append((inc, n))
                return
        raise RuntimeError("mark: no instruction to mark on " + eng)

    def vdrain(self, eng="vector"):
        for f in reversed(self.q[eng]):
            rec = getattr(f, "rec", None)
            if rec is not None:
                if rec["incs"]:
                    nm = rec["incs"][0][0]
                    self.wait(eng, nm, self.cnt[nm])
                else:
                    self.mark(eng, "vd")
                    self.wait(eng, "vd", self.cnt["vd"])
                return
        raise RuntimeError("vdrain: no instruction")

    def wait(self, eng, sname, val=None):
        v = self.cnt[sname] if val is None else val
        if v <= 0:
            return
        sems = self.sems

        def f(h, sname=sname, v=v):
            h.wait_ge(sems[sname], v)
        self.q[eng].append(f)

    def dma(self, eng, out, in_, inc=None, serial=True, **kw):
        if serial and inc is not None and self.cnt[inc] > 0:
            self.wait(eng, inc, self.cnt[inc])
        self.op(eng, "dma_start", inc=inc, incn=16, out=out, in_=in_, **kw)

    def barrier(self, tag):
        s = "bar_" + tag
        self.sem(s)
        for e in ENGINES:
            self.op(e, "nop", inc=s, incn=1)
        for e in ENGINES:
            self.wait(e, s, 5)

    def post_flag(self, slot, fidx):
        g, c = self.g, self.c
        pfl = self.sems["pfl"]

        def f(sp, slot=slot, fidx=fidx):
            KB._rc[0] += 1
            with sp.register("hr%d" % KB._rc[0]) as hr:
                sp.reg_load(hr, c["meta"][0:1, 0:1])
                with sp.If_eq(hr, 0):
                    sp.dma_start(out=g["flags"][0, slot:slot + 1].unsqueeze(0),
                                 in_=c["flagv"][0:1, fidx:fidx + 1]
                                 ).then_inc(pfl, 16)
                with sp.Else():
                    sp.dma_start(out=g["flags"][1, slot:slot + 1].unsqueeze(0),
                                 in_=c["flagv"][0:1, fidx:fidx + 1]
                                 ).then_inc(pfl, 16)
        self.q["sync"].append(f)

    def poll_flag(self, slot, value, inc=None):
        g, c = self.g, self.c
        pf = self.sems["pol"]

        def f(sp, slot=slot, value=value, inc=inc):
            KB._rc[0] += 1
            with sp.register("hr%d" % KB._rc[0]) as hr, \
                    sp.register("fv%d" % KB._rc[0]) as fv, \
                    sp.register("tgt%d" % KB._rc[0]) as tgt:
                sp.reg_load(hr, c["meta"][0:1, 0:1])
                sp.reg_mov(fv, 0)
                sp.reg_load(tgt, c["ptgt"][0:1, 0:1])
                with sp.While(lambda: sp.compute_val(fv) != value):
                    sp.reg_add(tgt, tgt, 16)
                    with sp.If_eq(hr, 0):
                        sp.dma_start(out=c["flin"][0:1, 0:1],
                                     in_=g["flags"][1, slot:slot + 1]
                                     .unsqueeze(0)).then_inc(pf, 16)
                    with sp.Else():
                        sp.dma_start(out=c["flin"][0:1, 0:1],
                                     in_=g["flags"][0, slot:slot + 1]
                                     .unsqueeze(0)).then_inc(pf, 16)
                    sp.wait_ge(pf, tgt)
                    sp.reg_load(fv, c["flin"][0:1, 0:1])
                sp.reg_save(c["ptgt"][0:1, 0:1], tgt)
        self.q["sync"].append(f)

    def quad_dma(self, eng, outs, ins, inc=None):
        c = self.c
        sems = self.sems
        if inc is not None:
            if self.cnt[inc] > 0:
                self.wait(eng, inc, self.cnt[inc])
            self.cnt[inc] += 16

        def f(h, outs=outs, ins=ins, inc=inc):
            KB._rc[0] += 1
            with h.register("qr%d" % KB._rc[0]) as hr, \
                    h.register("qp%d" % KB._rc[0]) as prr:
                h.reg_load(hr, c["meta"][0:1, 0:1])
                h.reg_load(prr, c["meta"][0:1, 1:2])
                for sv in range(2):
                    for pv in range(2):
                        idx = sv + 2 * pv
                        with h.If_eq(hr, sv):
                            with h.If_eq(prr, pv):
                                ins_i = h.dma_start(out=outs[idx],
                                                    in_=ins[idx])
                                if inc is not None:
                                    ins_i.then_inc(sems[inc], 16)
        self.q[eng].append(f)

    def half_dma(self, eng, out0, out1, in0, in1, inc=None, key=0):
        c = self.c
        sems = self.sems
        if inc is not None:
            if self.cnt[inc] > 0:
                self.wait(eng, inc, self.cnt[inc])
            self.cnt[inc] += 16

        def f(h, out0=out0, out1=out1, in0=in0, in1=in1, inc=inc, key=key):
            KB._rc[0] += 1
            with h.register("hr%d" % KB._rc[0]) as hr:
                h.reg_load(hr, c["meta"][0:1, key:key + 1])
                with h.If_eq(hr, 0):
                    ins = h.dma_start(out=out0, in_=in0)
                    if inc is not None:
                        ins.then_inc(sems[inc], 16)
                with h.Else():
                    ins = h.dma_start(out=out1, in_=in1)
                    if inc is not None:
                        ins.then_inc(sems[inc], 16)
        self.q[eng].append(f)

    def mm3(self, out_ap, whi, wlo, xhi, xlo, start, stop, tile_position=None):
        sems = self.sems
        rec = {"incs": []}

        def f(t, out_ap=out_ap, whi=whi, wlo=wlo, xhi=xhi, xlo=xlo,
              start=start, stop=stop, tp=tile_position, rec=rec):
            lkw = {} if tp is None else {"tile_position": tp}
            mkw = {} if tp is None else {"tile_position": tp,
                                         "skip_group_check": True}
            t.ldweights(whi, **lkw)
            t.matmul(out_ap, whi, xhi, start=start, stop=False, **mkw)
            t.matmul(out_ap, whi, xlo, start=False, stop=False, **mkw)
            t.ldweights(wlo, **lkw)
            ins = t.matmul(out_ap, wlo, xhi, start=False, stop=stop, **mkw)
            KB._emit_incs(t, ins, rec, sems)
        f.rec = rec
        self.q["tensor"].append(f)

    def play(self):
        with self.nc.Block() as block:
            for eng in ENGINES:
                ops = self.q[eng]

                def section(h, ops=ops):
                    for f in ops:
                        f(h)
                getattr(block, eng)(section)



DBG_SHAPES = {
    "mod": [1, 6 * D], "mod2": [1, 6 * D], "xa": [H, RNG], "h1": [512, D], "kfm": [128, RNG], "lrow": [H, RNG],
    "ys": [128, 4 * RNG], "ctx": [128, 4 * RNG], "x2": [512, D],
    "r1": [512, D], "attn": [128, 512], "xf1": [512, D], "wop": [512, D],
}


PHASES_LIMIT = 6
RACE_DETECT = False
DEV_NRNG = NRNG
DEV_P2STAGE = 4


def build_program(dbg_keys=()):
    nc = bass.Bass("TRN2", target_bir_lowering=False, debug=False,
                   detect_race_conditions=RACE_DETECT)
    g = {}

    def din(name, shape, dt=F32):
        g[name] = nc.dram_tensor(name, shape, dt, kind="ExternalInput").ap()

    def dscr(name, shape, dt=F32, shared=False):
        g[name] = nc.dram_tensor(name, shape, dt, kind="Internal",
                                 addr_space="Shared" if shared else "Local").ap()

    din("xb", [L, D]); din("c_col", [128, 8]); din("wmod", [D, 6 * D])
    din("WkT", [D, D]); din("WqT", [D, D])
    din("WvT", [D, NDIMS]); din("WgT", [D, NDIMS]); din("WbT", [D, NDIMS])
    din("WaT", [D, H]); din("dtb_col", [H, 1]); din("alog_col", [H, 1])
    din("normw_col", [128, 1]); din("WoT", [NDIMS, D]); din("bo_row", [1, D])
    din("n1w_row", [1, D]); din("n2w_row", [1, D])
    din("fqT", [D, NDIMS]); din("fkT", [D, NDIMS]); din("fvT", [D, NDIMS])
    din("owT", [NDIMS, D]); din("gateT", [D, 8]); din("upT", [D, ED])
    din("downT", [ED, D])
    din("ident", [128, 128]); din("incmask", [32, 512])
    din("resetm", [H, RNG]); din("bd64", [128, 2]); din("bd32", [128, 4])
    din("ones_row", [1, 128]); din("iotaQ", [128, 512]); din("qoffc", [128, 64])
    din("meta", [1, 4], I32)

    dscr("h1hi", [L, D], BF16); dscr("h1lo", [L, D], BF16)
    dscr("x2s", [L, D]); dscr("h2s", [L, D])
    dscr("r1hi", [L, D], BF16); dscr("r1lo", [L, D], BF16)
    dscr("Kfhi", [NDIMS, L], BF16); dscr("Kflo", [NDIMS, L], BF16)
    dscr("Qfhi", [NDIMS, L], BF16); dscr("Qflo", [NDIMS, L], BF16)
    dscr("Vaug", [L, FH * 65])
    dscr("r2hi_s", [2048, D], BF16); dscr("r2lo_s", [2048, D], BF16)
    dscr("uphi_s", [ED, 2048], BF16); dscr("uplo_s", [ED, 2048], BF16)
    dscr("xf1_s", [2048, D])
    dscr("woP", [2, L, D], shared=True)
    dscr("owP", [2, L, D], shared=True)
    dscr("ssX", [2, H, L], shared=True)
    dscr("flags", [2, 256], I32, shared=True)

    g["out_rows"] = nc.dram_tensor("out_rows", [2048, D], F32,
                                   kind="ExternalOutput").ap()
    for k in dbg_keys:
        g["dbg_" + k] = nc.dram_tensor("dbg_" + k, DBG_SHAPES[k], F32,
                                       kind="ExternalOutput").ap()

    kb = KB(nc, g)
    for s in ["cl", "mm", "vv", "vd", "sc", "dl", "wl", "st", "pol", "pfl",
              "gp",
              "s0", "s2", "sg", "s4", "s5", "s6", "s7", "s8", "s9",
              "sA", "sB", "sC", "sD", "sE", "fin"]:
        kb.sem(s)
    emit_all(kb)
    kb.play()
    return nc, g


def emit_all(kb):
    c = {}
    c["meta"] = kb.sb("c_meta", [1, 4], I32)
    c["ident"] = kb.sb("c_ident", [128, 128])
    c["bd64"] = kb.sb("c_bd64", [128, 2])
    c["bd32"] = kb.sb("c_bd32", [128, 4])
    c["ones_row"] = kb.sb("c_ones", [1, 128])
    c["normw"] = kb.sb("c_normw", [128, 1])
    c["dtb"] = kb.sb("c_dtb", [H, 1])
    c["negea"] = kb.sb("c_negea", [H, 1])
    c["incm"] = kb.sb("c_incm", [32, 512])
    c["resetm"] = kb.sb("c_resetm", [H, RNG])
    c["iotaQ"] = kb.sb("c_iotaQ", [128, 512])
    c["qoffc"] = kb.sb("c_qoffc", [128, 64])
    c["flagv"] = kb.sb("c_flagv", [1, 128], I32)
    bconst = kb.sb("c_bconst", [128, 8])
    BVALS = [EPS_LN, EPS_RMS, -LN8, 1e-9, 1.0]
    for i, v in enumerate(BVALS):
        kb.op("gpsimd", "memset", bconst[:, i:i + 1], float(v))
        kb.nc.const_aps.register(F32, float(v), bconst[:, i:i + 1])
    kb.mark("gpsimd", "gp")
    c["flin"] = kb.sb("c_flin", [1, 1], I32)
    c["ptgt"] = kb.sb("c_ptgt", [1, 1], I32)
    c["modrow"] = kb.sb("c_modrow", [1, 6 * D])
    c["n1w_row"] = kb.sb("c_n1wrow", [1, D])
    c["n2w_row"] = kb.sb("c_n2wrow", [1, D])
    c["bo_row"] = kb.sb("c_borow", [1, D])
    kb.c = c
    kb.pb = [kb.ps("pbank%d" % i, [128, 512]) for i in range(8)]
    kb.init_arena(39000)

    import os
    maxph = PHASES_LIMIT
    phase0(kb)
    if maxph >= 1:
        phase1(kb)
    if maxph >= 2:
        phase2(kb)
    if maxph >= 3:
        phase3(kb)
    if maxph >= 4:
        phase4(kb)
    if maxph >= 5:
        phase5(kb)
    if maxph >= 6:
        phase6(kb)
    if maxph < 6:
        kb.op("gpsimd", "memset", kb.c["flin"][:], 0)
        z = kb.sb("zout", [128, D])
        kb.op("gpsimd", "memset", z[:], 0.0, inc="gp")
        kb.wait("gpsimd", "gp", kb.cnt["gp"])
        for i in range(16):
            kb.dma("gpsimd", kb.g["out_rows"][128 * i:128 * i + 128, :], z[:],
                   inc="fin")
        kb.wait("gpsimd", "fin", kb.cnt["fin"])


def rep_rows(kb, src_row, dst_tile):
    c, pb = kb.c, kb.pb
    kb.wait("tensor", "dl", kb.cnt["dl"])
    for hq in range(2):
        bank = pb[hq]
        kb.wait("tensor", "vv", kb.cnt["vv"])
        kb.op("tensor", "matmul", bank[:, :], c["ones_row"][0:1, :],
              src_row[0:1, 512 * hq:512 * hq + 512], start=True, stop=True,
              inc="mm")
        kb.wait("vector", "mm", kb.cnt["mm"])
        kb.op("vector", "tensor_copy", dst_tile[:, 512 * hq:512 * hq + 512],
              bank[:, :], inc="vv")


def phase0(kb):
    g, c, pb = kb.g, kb.c, kb.pb
    loads = [("ident", c["ident"]), ("bd64", c["bd64"]), ("bd32", c["bd32"]),
             ("ones_row", c["ones_row"]), ("normw_col", c["normw"]),
             ("dtb_col", c["dtb"]), ("alog_col", c["negea"]),
             ("incmask", c["incm"]), ("resetm", c["resetm"]),
             ("iotaQ", c["iotaQ"]), ("qoffc", c["qoffc"]), ("meta", c["meta"])]
    for nm, dst in loads:
        kb.dma("sync", dst[:], g[nm][:, :], inc="cl")
    csil = kb.sb("csil", [128, 8])
    kb.dma("sync", csil[:], g["c_col"][:, :], inc="cl")
    kb.wait("gpsimd", "cl")
    for k in range(128):
        kb.op("gpsimd", "memset", c["flagv"][0:1, k:k + 1], MAGIC + k)
    kb.op("gpsimd", "memset", c["ptgt"][:], 0, inc="gp")

    kb.wait("scalar", "cl")
    kb.wait("scalar", "gp", 1)
    kb.op("scalar", "activation", csil[:], csil[:], AF.Silu, inc="sc")
    kb.op("scalar", "activation", c["negea"][:], c["negea"][:], AF.Exp,
          inc="sc")
    kb.wait("scalar", "sc", kb.cnt["sc"])
    kb.op("scalar", "activation", c["negea"][:], c["negea"][:], AF.Copy,
          scale=-1.0, inc="sc")

    modrow = c["modrow"]
    wmt = [kb.asb("wmt%d" % i, [128, 512]) for i in range(2)]
    for grp in range(12):
        for j in range(8):
            bi = j % 2
            kb.wait("sync", "s0", max(0, kb.cnt["s0"] - 1))
            kb.dma("sync", wmt[bi][:, :],
                   g["wmod"][128 * j:128 * j + 128,
                             512 * grp:512 * grp + 512], inc="dl")
            kb.wait("tensor", "sc", 2)
            kb.wait("tensor", "dl", kb.cnt["dl"])
            if grp > 1 and j == 0:
                kb.wait("tensor", "vv", max(0, kb.cnt["vv"] - 1))
            kb.op("tensor", "matmul", pb[grp % 2][0:1, :], csil[:, j:j + 1],
                  wmt[bi][:, :], start=(j == 0), stop=(j == 7))
            kb.mark("tensor", "s0")
        kb.wait("vector", "s0", kb.cnt["s0"])
        kb.op("vector", "tensor_copy",
              modrow[0:1, 512 * grp:512 * grp + 512],
              pb[grp % 2][0:1, :], inc="vv")
    tr = kb.asb("trrow", [1, D])
    for off in (2 * D, 5 * D):
        sl = modrow[0:1, off:off + D]
        kb.op("vector", "tensor_scalar", tr[0:1, :], sl, 0.0, None,
              op0=ALU.is_ge)
        kb.op("vector", "tensor_scalar", tr[0:1, :], tr[0:1, :], 2.0, -1.0,
              op0=ALU.mult, op1=ALU.add)
        kb.op("vector", "tensor_tensor", sl, sl, tr[0:1, :], ALU.mult)
        kb.op("vector", "tensor_scalar", sl, sl, 0.25, None, op0=ALU.max)
        kb.op("vector", "tensor_tensor", sl, sl, tr[0:1, :], ALU.mult)
    kb.op("vector", "tensor_scalar", modrow[0:1, D:2 * D],
          modrow[0:1, D:2 * D], 1.0, None, op0=ALU.add)
    kb.op("vector", "tensor_scalar", modrow[0:1, 4 * D:5 * D],
          modrow[0:1, 4 * D:5 * D], 1.0, None, op0=ALU.add)
    kb.mark("vector", "vv")
    if "dbg_mod" in g:
        kb.wait("gpsimd", "vv", kb.cnt["vv"])
        kb.dma("gpsimd", g["dbg_mod"][:, :], modrow[:, :], inc="st")
    if "dbg_mod2" in g:
        kb.barrier("dbgm")
        kb.dma("gpsimd", g["dbg_mod2"][:, :], modrow[:, :], inc="st")

    kb.dma("sync", c["n1w_row"][0:1, :], g["n1w_row"][:, :], inc="dl")
    kb.dma("sync", c["n2w_row"][0:1, :], g["n2w_row"][:, :], inc="dl")
    kb.dma("sync", c["bo_row"][0:1, :], g["bo_row"][:, :], inc="dl")
    kb.barrier("ph0")


def phase1(kb):
    g, c = kb.g, kb.c
    kb.areset()
    mod = {"ca1": kb.asb("m_ca1", [128, D]), "sa": kb.asb("m_sa", [128, D])}
    rep_rows(kb, c["modrow"][0:1, D:2 * D], mod["ca1"])
    rep_rows(kb, c["modrow"][0:1, 0:D], mod["sa"])
    xt = [kb.asb("p1x%d" % i, [128, D]) for i in range(2)]
    sq = kb.asb("p1sq", [128, D])
    hh = [kb.asb("p1h%d" % i, [128, D]) for i in range(2)]
    hhi = [kb.asb("p1hi%d" % i, [128, D], BF16) for i in range(2)]
    hlo = [kb.asb("p1lo%d" % i, [128, D], BF16) for i in range(2)]
    cols = kb.asb("p1cols", [128, 8])
    SUM, SSQ, NEGM, VAR, RSTD = 0, 1, 2, 3, 4

    vv_after = []
    for i in range(L // 128):
        bi = i % 2
        if "dbg_h1" in g:
            kb.wait("vector", "st", kb.cnt["st"])
        if i >= 2:
            kb.wait("sync", "vv", vv_after[i - 2])
        kb.dma("sync", xt[bi][:], g["xb"][128 * i:128 * i + 128, :], inc="dl")
        kb.wait("scalar", "dl", kb.cnt["dl"])
        kb.wait("scalar", "vv", kb.cnt["vv"])
        kb.op("scalar", "activation", sq[:], xt[bi][:], AF.Square,
              accum_out=cols[:, SSQ:SSQ + 1])
        kb.op("scalar", "activation", sq[:], xt[bi][:], AF.Copy,
              accum_out=cols[:, SUM:SUM + 1], inc="sc")
        kb.wait("vector", "sc", kb.cnt["sc"])
        kb.op("vector", "tensor_scalar", cols[:, NEGM:NEGM + 1],
              cols[:, SUM:SUM + 1], -1.0 / D, None, op0=ALU.mult)
        kb.vdrain()
        kb.op("vector", "tensor_tensor", cols[:, VAR:VAR + 1],
              cols[:, NEGM:NEGM + 1], cols[:, NEGM:NEGM + 1], ALU.mult)
        kb.op("vector", "tensor_scalar", cols[:, 5:6],
              cols[:, SSQ:SSQ + 1], 1.0 / D, None, op0=ALU.mult)
        kb.vdrain()
        kb.op("vector", "tensor_tensor", cols[:, VAR:VAR + 1],
              cols[:, 5:6], cols[:, VAR:VAR + 1], ALU.subtract, inc="vv")
        kb.wait("scalar", "vv", kb.cnt["vv"])
        kb.op("scalar", "activation", cols[:, RSTD:RSTD + 1],
              cols[:, VAR:VAR + 1], AF.Ln, bias=EPS_LN, inc="sc")
        kb.wait("scalar", "sc", kb.cnt["sc"])
        kb.op("scalar", "activation", cols[:, RSTD:RSTD + 1],
              cols[:, RSTD:RSTD + 1], AF.Exp, scale=-0.5, inc="sc")
        kb.wait("vector", "sc", kb.cnt["sc"])
        kb.op("vector", "scalar_tensor_tensor", hh[bi][:], xt[bi][:],
              cols[:, NEGM:NEGM + 1], mod["ca1"][:, :],
              op0=ALU.add, op1=ALU.mult)
        kb.op("vector", "scalar_tensor_tensor", hh[bi][:], hh[bi][:],
              cols[:, RSTD:RSTD + 1], mod["sa"][:, :],
              op0=ALU.mult, op1=ALU.add)
        kb.wait("vector", "wl", max(0, kb.cnt["wl"] - 32))
        kb.op("vector", "tensor_copy", hhi[bi][:], hh[bi][:])
        kb.op("vector", "tensor_sub", hlo[bi][:], hh[bi][:], hhi[bi][:],
              inc="vv")
        kb.wait("gpsimd", "vv", kb.cnt["vv"])
        kb.dma("gpsimd", g["h1hi"][128 * i:128 * i + 128, :], hhi[bi][:],
               inc="wl")
        kb.dma("gpsimd", g["h1lo"][128 * i:128 * i + 128, :], hlo[bi][:],
               inc="wl")
        vv_after.append(kb.cnt["vv"])
        if "dbg_h1" in g and i < 4:
            kb.dma("gpsimd", g["dbg_h1"][128 * i:128 * i + 128, :], hh[bi][:],
                   inc="st")
    kb.barrier("ph1")


def split_weight_to_scratch(kb, src_name, width, hi_name, lo_name):
    g = kb.g
    stg, shi, slo = kb.wstg, kb.whis, kb.wlos
    for j in range(D // 128):
        kb.wait("sync", "vv", kb.cnt["vv"])
        kb.dma("sync", stg[:, 0:width], g[src_name][128 * j:128 * j + 128, :],
               inc="dl")
        kb.wait("vector", "dl", kb.cnt["dl"])
        kb.wait("vector", "wl", kb.cnt["wl"])
        kb.op("vector", "tensor_copy", shi[:, 0:width], stg[:, 0:width])
        kb.vdrain()
        kb.op("vector", "tensor_sub", slo[:, 0:width], stg[:, 0:width],
              shi[:, 0:width], inc="vv")
        kb.wait("gpsimd", "vv", kb.cnt["vv"])
        kb.dma("gpsimd", g[hi_name][128 * j:128 * j + 128, :], shi[:, 0:width],
               inc="wl")
        kb.dma("gpsimd", g[lo_name][128 * j:128 * j + 128, :], slo[:, 0:width],
               inc="wl")


def phase2(kb):
    g, c, pb = kb.g, kb.c, kb.pb
    nc = kb.nc
    for nm, w in [("WkT", D), ("WqT", D), ("WvT", NDIMS), ("WgT", NDIMS),
                  ("WbT", NDIMS), ("WaT", H)]:
        g[nm + "hi"] = nc.dram_tensor(nm + "hi", [D, w], BF16,
                                      kind="Internal").ap()
        g[nm + "lo"] = nc.dram_tensor(nm + "lo", [D, w], BF16,
                                      kind="Internal").ap()
    kb.areset()
    kb.wstg = kb.asb("wstg", [128, D], tail=True)
    kb.whis = kb.asb("whis", [128, D], BF16, tail=True)
    kb.wlos = kb.asb("wlos", [128, D], BF16, tail=True)
    for nm, w in [("WkT", D), ("WqT", D), ("WvT", NDIMS), ("WgT", NDIMS),
                  ("WbT", NDIMS), ("WaT", H)]:
        split_weight_to_scratch(kb, nm, w, nm + "hi", nm + "lo")
    if DEV_P2STAGE < 2:
        kb.barrier("p2dev")
        return
    wo_hi = [kb.asb("wo_hi%d" % q, [128, D], BF16) for q in range(4)]
    wo_lo = [kb.asb("wo_lo%d" % q, [128, D], BF16) for q in range(4)]
    for q in range(4):
        kb.wait("sync", "vv", kb.cnt["vv"])
        kb.dma("sync", kb.wstg[:, :], g["WoT"][128 * q:128 * q + 128, :],
               inc="dl")
        kb.wait("vector", "dl", kb.cnt["dl"])
        kb.op("vector", "tensor_copy", wo_hi[q][:, :], kb.wstg[:, :])
        kb.op("vector", "tensor_sub", wo_lo[q][:, :], kb.wstg[:, :],
              wo_hi[q][:, :], inc="vv")
    kb.barrier("p2w")

    h1f_hi = [kb.asb("h1fh%d" % j, [128, RNG], BF16) for j in range(8)]
    h1f_lo = [kb.asb("h1fl%d" % j, [128, RNG], BF16) for j in range(8)]
    wbh = [kb.asb("wbh%d" % i, [128, 512], BF16) for i in range(2)]
    wbl = [kb.asb("wbl%d" % i, [128, 512], BF16) for i in range(2)]
    Kfm = [kb.asb("Kfm%d" % j, [128, RNG]) for j in range(8)]
    Qn = [kb.asb("Qn%d" % j, [128, RNG]) for j in range(8)]
    Kbar = [kb.asb("Kbar%d" % j, [128, RNG]) for j in range(8)]
    Vt = kb.asb("Vt", [128, 4 * RNG])
    Gs = kb.asb("Gs", [128, 4 * RNG])
    negB = kb.asb("negB", [128, 4 * RNG])
    rep_a = kb.asb("rep_a", [128, 4 * RNG])
    repbuf = kb.asb("repbuf", [128, RNG])
    sqbuf = kb.asb("sqbuf", [128, RNG])
    knqT = kb.asb("knqT", [128, 8 * RNG], tail=True)
    kn2 = knqT[0:2]
    qn2 = knqT[32:34]
    lrow = kb.asb("lrow", [H, RNG])
    arow = kb.asb("arow", [H, RNG])
    acrow = kb.asb("acrow", [H, RNG])
    sprow = kb.asb("sprow", [H, RNG])
    aF = nc.dram_tensor("aF", [1, H * RNG], F32, kind="Internal").ap()
    acF = nc.dram_tensor("acF", [1, H * RNG], F32, kind="Internal").ap()
    knF = nc.dram_tensor("knF", [1, 2 * 8 * RNG], F32, kind="Internal").ap()
    qnF = nc.dram_tensor("qnF", [1, 2 * 8 * RNG], F32, kind="Internal").ap()
    rsF = nc.dram_tensor("rsF", [1, H * RNG], F32, kind="Internal").ap()
    DtFd = nc.dram_tensor("DtFd", [1, H * CH * CH], F32, kind="Internal").ap()
    GAFd = nc.dram_tensor("GAFd", [1, H * CH * CH], F32, kind="Internal").ap()
    acg = kb.asb("acg", [128, 8 * NCH])
    acl = kb.asb("acl", [H, NCH])
    aclF = nc.dram_tensor("aclF", [1, H * NCH], F32, kind="Internal").ap()
    Dt = [kb.asb("Dt%d" % i, [H, CH * CH]) for i in range(1)]
    A16g = kb.asb("A16g", [H, CH * CH])
    REPA = [kb.asb("REPA%d" % i, [128, 4096]) for i in range(1)]
    eDBs = [kb.asb("eDBs%d" % i, [32, 512]) for i in range(1)]
    Btm = [kb.asb("Btm%d" % i, [32, 512]) for i in range(1)]
    Gq = [kb.asb("Gq%d" % i, [128, 128]) for i in range(1)]
    P0 = kb.asb("P0t", [128, 128])
    Ut = kb.asb("Ut", [128, 128])
    ACC = kb.asb("ACCt", [128, 128])
    tmpu = kb.asb("tmpu", [128, 4])
    tmpa = kb.asb("tmpa", [128, 128])
    tmpa3 = tmpa[:, 0:128].rearrange("p (q t) -> p q t", q=4)
    Utm = kb.asb("Utm", [32, 512])
    Ktm = kb.asb("Ktm", [32, 1024])
    S_all = kb.asb("S_all", [128, 256])
    ys = kb.asb("ysr", [128, 4 * RNG])
    ctxhi = kb.asb("ctxhi", [128, 4 * RNG], BF16)
    ctxlo = kb.asb("ctxlo", [128, 4 * RNG], BF16)
    ssh = kb.asb("ssh", [4, 4 * RNG])
    ssxA = kb.asb("ssxA", [H, RNG])
    ssxB = kb.asb("ssxB", [H, RNG])
    rstdy = kb.asb("rstdy", [H, RNG])
    rsy_rep = kb.asb("rsy_rep", [128, 4 * RNG])
    woev = kb.asb("woev", [128, 512])

    Vt3 = Vt[:, :].rearrange("p (q t) -> p q t", q=4)
    Gs3 = Gs[:, :].rearrange("p (q t) -> p q t", q=4)
    nB3 = negB[:, :].rearrange("p (q t) -> p q t", q=4)
    ra3 = rep_a[:, :].rearrange("p (q t) -> p q t", q=4)
    ys3 = ys[:, :].rearrange("p (q t) -> p q t", q=4)
    rsy3 = rsy_rep[:, :].rearrange("p (q t) -> p q t", q=4)
    P03 = P0[:, :].rearrange("p (q t) -> p q t", q=4)
    U3 = Ut[:, :].rearrange("p (q t) -> p q t", q=4)
    A3 = ACC[:, :].rearrange("p (q t) -> p q t", q=4)
    REPA3 = [t[:, :].rearrange("p (q f) -> p q f", q=4) for t in REPA]

    kb.op("gpsimd", "memset", S_all[:], 0.0, inc="gp")

    def proj(whiname, wloname, ngrp, nocs, evac):
        for grp in range(ngrp):
            kb.wait("tensor", "s6", kb.cnt["s6"])
            for j in range(8):
                bi = j % 2
                kb.wait("sync", "s0", max(0, kb.cnt["s0"] - 1))
                cw = 512 if ngrp > 1 else (nocs * 128)
                kb.dma("sync", wbh[bi][:, 0:cw],
                       g[whiname][128 * j:128 * j + 128,
                                  grp * 512:grp * 512 + cw], inc="dl")
                kb.dma("sync", wbl[bi][:, 0:cw],
                       g[wloname][128 * j:128 * j + 128,
                                  grp * 512:grp * 512 + cw], inc="dl")
                kb.wait("tensor", "dl", kb.cnt["dl"])
                for i in range(nocs):
                    kb.mm3(pb[i][:, 0:RNG],
                           wbh[bi][:, 128 * i:128 * i + 128],
                           wbl[bi][:, 128 * i:128 * i + 128],
                           h1f_hi[j][:, :], h1f_lo[j][:, :],
                           start=(j == 0), stop=(j == 7))
                kb.mark("tensor", "s0")
            for i in range(nocs):
                evac(grp * nocs + i, pb[i])

    kb.wait("sync", "wl", kb.cnt["wl"])
    kb.wait("scalar", "wl", kb.cnt["wl"])
    for r in range(DEV_NRNG):
        t0 = RNG * r
        if "dbg_ys" in g or "dbg_ctx" in g or "dbg_kfm" in g:
            kb.wait("vector", "st", kb.cnt["st"])
        kb.wait("scalar", "s0", kb.cnt["s0"])
        for j in range(8):
            kb.dma("scalar", h1f_hi[j][:, :],
                   g["h1hi"][t0:t0 + RNG, 128 * j:128 * j + 128],
                   transpose=True, inc="s5")
            kb.dma("scalar", h1f_lo[j][:, :],
                   g["h1lo"][t0:t0 + RNG, 128 * j:128 * j + 128],
                   transpose=True, inc="s5")
        if DEV_P2STAGE == 15:
            continue
        kb.wait("tensor", "s5", kb.cnt["s5"])

        def evk(oc, bank):
            kb.wait("vector", "s0", kb.cnt["s0"])
            kb.op("vector", "tensor_copy", Kfm[oc][:, :], bank[:, 0:RNG],
                  inc="s6")

        def evq(oc, bank):
            kb.wait("vector", "s0", kb.cnt["s0"])
            kb.op("vector", "tensor_copy", Qn[oc][:, :], bank[:, 0:RNG],
                  inc="s6")

        def evv(oc, bank):
            kb.wait("vector", "s0", kb.cnt["s0"])
            kb.op("vector", "tensor_copy", Vt3[:, oc, :], bank[:, 0:RNG],
                  inc="s6")

        def evg(oc, bank):
            kb.wait("scalar", "s0", kb.cnt["s0"])
            kb.op("scalar", "activation", Gs3[:, oc, :], bank[:, 0:RNG],
                  AF.Silu, inc="s6")

        def evb(oc, bank):
            kb.wait("scalar", "s0", kb.cnt["s0"])
            kb.op("scalar", "activation", nB3[:, oc, :], bank[:, 0:RNG],
                  AF.Sigmoid, inc="s6")

        proj("WkThi", "WkTlo", 2, 4, evk)
        if DEV_P2STAGE == 17:
            continue
        proj("WqThi", "WqTlo", 2, 4, evq)
        proj("WvThi", "WvTlo", 1, 4, evv)
        proj("WgThi", "WgTlo", 1, 4, evg)
        proj("WbThi", "WbTlo", 1, 4, evb)
        if DEV_P2STAGE == 19:
            continue
        for j in range(8):
            bi = j % 2
            kb.wait("sync", "s0", max(0, kb.cnt["s0"] - 1))
            kb.dma("sync", wbh[bi][:, 0:H],
                   g["WaThi"][128 * j:128 * j + 128, :], inc="dl")
            kb.dma("sync", wbl[bi][:, 0:H],
                   g["WaTlo"][128 * j:128 * j + 128, :], inc="dl")
            kb.wait("tensor", "dl", kb.cnt["dl"])
            kb.mm3(pb[4][:, 0:RNG], wbh[bi][:, 0:128], wbl[bi][:, 0:128],
                   h1f_hi[j][:, :], h1f_lo[j][:, :],
                   start=(j == 0), stop=(j == 7))
            kb.mark("tensor", "s0")
        if DEV_P2STAGE == 2:
            continue
        kb.wait("scalar", "s0", kb.cnt["s0"])
        kb.op("scalar", "activation", sprow[:, :], pb[4][0:H, 0:RNG],
              AF.Exp, bias=c["dtb"][:, 0:1], inc="sc")
        if "dbg_xa" in g and r == 0:
            kb.wait("gpsimd", "sc", kb.cnt["sc"])
            kb.dma("gpsimd", g["dbg_xa"][:, :], sprow[:, :], inc="st")
            kb.wait("scalar", "st", kb.cnt["st"])
        kb.wait("scalar", "sc", kb.cnt["sc"])
        kb.op("scalar", "activation", sprow[:, :], sprow[:, :], AF.Ln,
              bias=1.0, inc="sc")
        kb.wait("vector", "sc", kb.cnt["sc"])
        kb.op("vector", "tensor_scalar", sprow[:, :], sprow[:, :],
              c["negea"][:, 0:1], None, op0=ALU.mult)
        kb.op("vector", "tensor_tensor_scan", lrow[:, :], c["resetm"][:, :],
              sprow[:, :], 0.0, op0=ALU.mult, op1=ALU.add, inc="vv")
        kb.wait("scalar", "vv", kb.cnt["vv"])
        kb.op("scalar", "activation", arow[:, :], lrow[:, :], AF.Exp, inc="sc")
        lce = lrow[:, CH - 1::CH]
        kb.wait("vector", "sc", kb.cnt["sc"])
        kb.op("vector", "tensor_tensor",
              sprow[:, :].rearrange("h (cc t) -> h cc t", cc=NCH),
              lce.unsqueeze(2).broadcast_to([H, NCH, CH]),
              lrow[:, :].rearrange("h (cc t) -> h cc t", cc=NCH),
              ALU.subtract, inc="vv")
        kb.wait("scalar", "vv", kb.cnt["vv"])
        kb.op("scalar", "activation", acrow[:, :], sprow[:, :], AF.Exp,
              inc="sc")
        if "dbg_lrow" in g and r == 0:
            kb.wait("gpsimd", "sc", kb.cnt["sc"])
            kb.dma("gpsimd", g["dbg_lrow"][:, :], lrow[:, :], inc="st")
        kb.wait("gpsimd", "sc", kb.cnt["sc"])
        kb.wait("gpsimd", "s7", kb.cnt["s7"])
        kb.dma("gpsimd",
               aF[0:1, :].rearrange("o (h t) -> (o h) t", h=H), arow[:, :],
               inc="s8")
        kb.dma("gpsimd",
               acF[0:1, :].rearrange("o (h t) -> (o h) t", h=H), acrow[:, :],
               inc="s8")

        kb.wait("vector", "s6", kb.cnt["s6"])
        for oc in range(8):
            kb.op("vector", "tensor_tensor", sqbuf[:, :], Kfm[oc][:, :],
                  Kfm[oc][:, :], ALU.mult, inc="vv")
            kb.wait("tensor", "vv", kb.cnt["vv"])
            kb.op("tensor", "matmul", pb[5][0:2, 0:RNG], c["bd64"][:, :],
                  sqbuf[:, :], start=True, stop=True, inc="mm")
            kb.wait("vector", "mm", kb.cnt["mm"])
            kb.op("vector", "tensor_copy", kn2[:, RNG * oc:RNG * oc + RNG],
                  pb[5][0:2, 0:RNG])
            kb.op("vector", "tensor_tensor", sqbuf[:, :], Qn[oc][:, :],
                  Qn[oc][:, :], ALU.mult, inc="vv")
            kb.wait("tensor", "vv", kb.cnt["vv"])
            kb.op("tensor", "matmul", pb[5][0:2, 0:RNG], c["bd64"][:, :],
                  sqbuf[:, :], start=True, stop=True, inc="mm")
            kb.wait("vector", "mm", kb.cnt["mm"])
            kb.op("vector", "tensor_copy", qn2[:, RNG * oc:RNG * oc + RNG],
                  pb[5][0:2, 0:RNG], inc="vv")
        kb.wait("scalar", "vv", kb.cnt["vv"])
        kb.op("scalar", "activation", kn2[:, :], kn2[:, :], AF.Ln)
        kb.op("scalar", "activation", qn2[:, :], qn2[:, :], AF.Ln, inc="sc")
        kb.wait("scalar", "sc", kb.cnt["sc"])
        kb.op("scalar", "activation", kn2[:, :], kn2[:, :], AF.Exp, scale=-0.5)
        kb.op("scalar", "activation", qn2[:, :], qn2[:, :], AF.Exp,
              scale=-0.5, bias=-LN8, inc="sc")
        kb.wait("gpsimd", "sc", kb.cnt["sc"])
        kb.wait("gpsimd", "s7", kb.cnt["s7"])
        kb.dma("gpsimd", knF[0:1, :].rearrange("o (a t) -> (o a) t", a=2),
               kn2[:, :], inc="s8")
        kb.dma("gpsimd", qnF[0:1, :].rearrange("o (a t) -> (o a) t", a=2),
               qn2[:, :], inc="s8")
        kb.wait("sync", "s8", kb.cnt["s8"])
        for oc in range(8):
            kb.dma("sync", repbuf[:, :],
                   knF[0:1, :]
                   .rearrange("o (a oc2 t) -> o a oc2 t", a=2, oc2=8)
                   [0, :, oc, :].unsqueeze(1).broadcast_to([2, 64, RNG]),
                   inc="s7")
            kb.wait("vector", "s7", kb.cnt["s7"])
            kb.op("vector", "tensor_tensor", Kfm[oc][:, :], Kfm[oc][:, :],
                  repbuf[:, :], ALU.mult, inc="vv")
            kb.wait("sync", "vv", kb.cnt["vv"])
            kb.dma("sync", repbuf[:, :],
                   qnF[0:1, :]
                   .rearrange("o (a oc2 t) -> o a oc2 t", a=2, oc2=8)
                   [0, :, oc, :].unsqueeze(1).broadcast_to([2, 64, RNG]),
                   inc="s7")
            kb.wait("vector", "s7", kb.cnt["s7"])
            kb.op("vector", "tensor_tensor", Qn[oc][:, :], Qn[oc][:, :],
                  repbuf[:, :], ALU.mult, inc="vv")
            kb.wait("sync", "vv", kb.cnt["vv"])
            kb.dma("sync", repbuf[:, :],
                   acF[0:1, :].rearrange("o (h t) -> o h t", h=H)
                   [0, 2 * oc:2 * oc + 2, :].unsqueeze(1)
                   .broadcast_to([2, 64, RNG]), inc="s7")
            kb.wait("vector", "s7", kb.cnt["s7"])
            kb.op("vector", "tensor_tensor", Kbar[oc][:, :], Kfm[oc][:, :],
                  repbuf[:, :], ALU.mult, inc="vv")
            kb.wait("sync", "vv", kb.cnt["vv"])
        kb.wait("vector", "s6", kb.cnt["s6"])
        kb.op("vector", "tensor_scalar", negB[:, :], negB[:, :], -1.0,
              None, op0=ALU.mult, inc="vv")
        for qv in range(4):
            kb.dma("sync", ra3[:, qv, :],
                   bass.AP(aF.tensor, 4 * qv * RNG,
                           [[RNG, 4], [0, EH], [1, RNG]]), inc="s7")
        kb.wait("vector", "sc", kb.cnt["sc"])
        kb.op("vector", "tensor_copy", acl[:, :], arow[:, CH - 1::CH],
              inc="vv")
        kb.wait("gpsimd", "vv", kb.cnt["vv"])
        kb.wait("gpsimd", "s7", kb.cnt["s7"])
        kb.dma("gpsimd", aclF[0:1, :].rearrange("o (h cc) -> (o h) cc", h=H),
               acl[:, :], inc="s8")
        kb.wait("sync", "s8", kb.cnt["s8"])
        for jv in range(8):
            kb.dma("sync", acg[:, NCH * jv:NCH * jv + NCH],
                   bass.AP(aclF.tensor, 2 * jv * NCH,
                           [[NCH, 2], [0, 64], [1, NCH]]), inc="s7")
        if "dbg_kfm" in g and r == 0:
            kb.wait("gpsimd", "vv", kb.cnt["vv"])
            kb.dma("gpsimd", g["dbg_kfm"][:, :], Kfm[0][:, :], inc="st")

        if DEV_P2STAGE == 25:
            continue
        for li in range(NCH):
            par = 0
            cc0 = CH * li
            kb.wait("tensor", "s7", kb.cnt["s7"])
            kb.wait("tensor", "vv", kb.cnt["vv"])
            kb.wait("tensor", "s9", max(0, kb.cnt["s9"] - 1))
            kb.wait("tensor", "sD", kb.cnt["sD"])
            for q in range(4):
                for hl in range(4):
                    h = 4 * q + hl
                    j, hl2 = h // 2, h % 2
                    kb.op("tensor", "matmul",
                          pb[6][32 * hl:32 * hl + 32, 32 * q:32 * q + 32],
                          Kfm[j][64 * hl2:64 * hl2 + 64, cc0:cc0 + CH],
                          Kfm[j][64 * hl2:64 * hl2 + 64, cc0:cc0 + CH],
                          start=True, stop=True,
                          tile_position=(64 * hl2, 32 * hl),
                          skip_group_check=True)
            kb.mark("tensor", "mm")
            for h in range(H):
                j, hl2 = h // 2, h % 2
                kb.op("tensor", "matmul",
                      (pb[7] if hl2 == 0 else pb[2])[0:32,
                                                     32 * h:32 * h + 32],
                      Kfm[j][64 * hl2:64 * hl2 + 64, cc0:cc0 + CH],
                      Qn[j][64 * hl2:64 * hl2 + 64, cc0:cc0 + CH],
                      start=True, stop=True,
                      tile_position=(64 * hl2, 0), skip_group_check=True)
            kb.mark("tensor", "mm")
            if DEV_P2STAGE == 311:
                continue
            lsl = lrow[:, cc0:cc0 + CH]
            kb.wait("vector", "sC", max(0, kb.cnt["sC"] - 1))
            kb.op("vector", "tensor_tensor",
                  Dt[par][:, :].rearrange("h (i jj) -> h i jj", i=CH),
                  lsl.unsqueeze(1).broadcast_to([H, CH, CH]),
                  lsl.unsqueeze(2).broadcast_to([H, CH, CH]),
                  ALU.subtract)
            kb.op("vector", "tensor_scalar", Dt[par][:, :], Dt[par][:, :],
                  0.0, None, op0=ALU.min, inc="vv")
            kb.wait("scalar", "vv", kb.cnt["vv"])
            kb.op("scalar", "activation", Dt[par][:, :], Dt[par][:, :],
                  AF.Exp, inc="sc")
            kb.wait("gpsimd", "sc", kb.cnt["sc"])
            kb.wait("gpsimd", "s7", kb.cnt["s7"])
            kb.dma("gpsimd",
                   DtFd[0:1, :].rearrange("o (h f) -> (o h) f", h=H),
                   Dt[par][:, :], inc="s8")
            if DEV_P2STAGE == 312:
                continue
            kb.wait("vector", "mm", kb.cnt["mm"])
            kb.op("vector", "tensor_copy", Gq[par][:, :], pb[6][:, 0:128],
                  inc="vv")
            kb.wait("gpsimd", "vv", kb.cnt["vv"])
            kb.wait("gpsimd", "s9", kb.cnt["s9"])
            for q in range(4):
                for hl in range(4):
                    h = 4 * q + hl
                    kb.dma("gpsimd",
                           GAFd[0:1, 1024 * h:1024 * h + 1024]
                           .rearrange("o (ss tt) -> (o ss) tt", ss=CH),
                           Gq[par][32 * hl:32 * hl + 32, 32 * q:32 * q + 32],
                           inc="s8")
            kb.wait("sync", "s8", kb.cnt["s8"])
            kb.dma("sync", A16g[:, :],
                   GAFd[0:1, :].rearrange("o (h f) -> (o h) f", h=H),
                   inc="s7")
            kb.wait("vector", "s7", kb.cnt["s7"])
            kb.wait("vector", "sc", kb.cnt["sc"])
            kb.op("vector", "tensor_tensor", A16g[:, :], A16g[:, :],
                  Dt[par][:, :], ALU.mult, inc="vv")
            kb.wait("gpsimd", "vv", kb.cnt["vv"])
            kb.dma("gpsimd",
                   GAFd[0:1, :].rearrange("o (h f) -> (o h) f", h=H),
                   A16g[:, :], inc="s8")
            if DEV_P2STAGE == 313:
                continue
            kb.wait("sync", "s8", kb.cnt["s8"])
            for qv in range(4):
                kb.dma("sync", REPA3[par][:, qv, :],
                       bass.AP(GAFd.tensor, 4 * qv * 1024,
                               [[1024, 4], [0, EH], [1, 1024]]), inc="s9")
            if DEV_P2STAGE == 314:
                continue
            kb.wait("sync", "s8", kb.cnt["s8"])
            kb.dma("sync",
                   eDBs[par][:, :].rearrange("s (h t) -> s h t", h=H),
                   bass.AP(DtFd.tensor, 0,
                           [[CH, CH], [1024, H], [1, CH]]),
                   inc="s7")
            kb.wait("vector", "s7", kb.cnt["s7"])
            kb.wait("vector", "mm", kb.cnt["mm"])
            B3v = Btm[par][:, :].rearrange("s (hh t) -> s hh t", hh=H)
            e3v = eDBs[par][:, :].rearrange("s (hh t) -> s hh t", hh=H)
            kb.op("vector", "tensor_tensor", B3v[:, 0::2, :],
                  pb[7][0:32, :].rearrange("s (hh t) -> s hh t", hh=H)
                  [:, 0::2, :], e3v[:, 0::2, :], ALU.mult)
            kb.op("vector", "tensor_tensor", B3v[:, 1::2, :],
                  pb[2][0:32, :].rearrange("s (hh t) -> s hh t", hh=H)
                  [:, 1::2, :], e3v[:, 1::2, :], ALU.mult)
            kb.op("vector", "tensor_tensor", Btm[par][:, :], Btm[par][:, :],
                  c["incm"][:, :], ALU.mult, inc="sA")
            if DEV_P2STAGE == 31:
                continue

            kb.wait("tensor", "sA", kb.cnt["sA"])
            for q in range(4):
                for hl in range(4):
                    h = 4 * q + hl
                    j, hl2 = h // 2, h % 2
                    kb.op("tensor", "matmul",
                          pb[5][32 * hl:32 * hl + 32, 32 * q:32 * q + 32],
                          S_all[64 * hl2:64 * hl2 + 64, 32 * j:32 * j + 32],
                          Kfm[j][64 * hl2:64 * hl2 + 64, cc0:cc0 + CH],
                          start=True, stop=True,
                          tile_position=(64 * hl2, 32 * hl),
                          skip_group_check=True)
            kb.mark("tensor", "mm")
            pb5v = pb[5][:, 0:128].rearrange("p (q t) -> p q t", q=4)
            kb.wait("vector", "mm", kb.cnt["mm"])
            kb.op("vector", "tensor_tensor", tmpa3,
                  ra3[:, :, cc0:cc0 + CH],
                  pb[5][:, 0:128].rearrange("p (q t) -> p q t", q=4), ALU.mult)
            kb.vdrain()
            kb.op("vector", "tensor_tensor", tmpa3, tmpa3,
                  Vt3[:, :, cc0:cc0 + CH], ALU.subtract)
            kb.vdrain()
            kb.op("vector", "tensor_tensor",
                  P0[:, :].rearrange("p (q t) -> p q t", q=4), tmpa3,
                  nB3[:, :, cc0:cc0 + CH], ALU.mult)
            kb.op("vector", "memset", ACC[:, :], 0.0)
            kb.vdrain()
            if DEV_P2STAGE == 32:
                continue
            kb.wait("vector", "s9", kb.cnt["s9"])
            for t in range(CH):
                kb.op("vector", "tensor_tensor", tmpu[:, :],
                      A3[:, :, t:t + 1].rearrange("p q o -> p (q o)"),
                      nB3[:, :, cc0 + t:cc0 + t + 1]
                      .rearrange("p q o -> p (q o)"), ALU.mult)
                kb.vdrain()
                kb.op("vector", "tensor_tensor",
                      U3[:, :, t:t + 1].rearrange("p q o -> p (q o)"),
                      tmpu[:, :],
                      P03[:, :, t:t + 1].rearrange("p q o -> p (q o)"),
                      ALU.add, inc=("sB" if t == CH - 1 else None))
                kb.vdrain()
                if t < CH - 1:
                    nrem = CH - 1 - t
                    kb.op("vector", "tensor_tensor",
                          tmpa[:, 0:4 * nrem]
                          .rearrange("p (q n) -> p q n", q=4),
                          U3[:, :, t:t + 1].broadcast_to([128, 4, nrem]),
                          REPA3[par][:, :, 33 * t + 1:33 * t + 1 + nrem],
                          ALU.mult)
                    kb.vdrain()
                    kb.op("vector", "tensor_tensor",
                          A3[:, :, t + 1:t + 1 + nrem],
                          tmpa[:, 0:4 * nrem]
                          .rearrange("p (q n) -> p q n", q=4),
                          A3[:, :, t + 1:t + 1 + nrem], ALU.add)
                    kb.vdrain()
            if DEV_P2STAGE == 33:
                continue
            kb.wait("tensor", "sB", kb.cnt["sB"])
            for q in range(4):
                kb.op("tensor", "transpose",
                      pb[4][0:32, 128 * q:128 * q + 128],
                      Ut[:, 32 * q:32 * q + 32], c["ident"][:, :])
            kb.mark("tensor", "mm")
            kb.wait("vector", "mm", kb.cnt["mm"])
            kb.op("vector", "tensor_copy", Utm[:, :], pb[4][0:32, :], inc="sC")
            for jj in range(2):
                kb.wait("tensor", "sC", kb.cnt["sC"])
                for j4 in range(4):
                    j = 4 * jj + j4
                    kb.op("tensor", "transpose",
                          pb[4][0:32, 128 * j4:128 * j4 + 128],
                          Kbar[j][:, cc0:cc0 + CH], c["ident"][:, :])
                kb.mark("tensor", "mm")
                kb.wait("vector", "mm", kb.cnt["mm"])
                kb.op("vector", "tensor_copy",
                      Ktm[:, 512 * jj:512 * jj + 512], pb[4][0:32, :],
                      inc="sC")
            if DEV_P2STAGE == 34:
                continue
            kb.wait("tensor", "sC", kb.cnt["sC"])
            for q in range(4):
                for hl in range(4):
                    h = 4 * q + hl
                    j, hl2 = h // 2, h % 2
                    kb.op("tensor", "matmul",
                          pb[6][32 * hl:32 * hl + 32, 32 * q:32 * q + 32],
                          S_all[64 * hl2:64 * hl2 + 64, 32 * j:32 * j + 32],
                          Qn[j][64 * hl2:64 * hl2 + 64, cc0:cc0 + CH],
                          start=True, stop=True,
                          tile_position=(64 * hl2, 32 * hl),
                          skip_group_check=True)
            for q in range(4):
                for hl in range(4):
                    h = 4 * q + hl
                    kb.op("tensor", "matmul",
                          pb[3][32 * hl:32 * hl + 32, 32 * q:32 * q + 32],
                          Utm[0:32, 128 * q + 32 * hl:128 * q + 32 * hl + 32],
                          Btm[par][0:32, 32 * h:32 * h + 32],
                          start=True, stop=True,
                          tile_position=(0, 32 * hl), skip_group_check=True)
            kb.mark("tensor", "mm")
            kb.wait("vector", "mm", kb.cnt["mm"])
            kb.op("vector", "tensor_tensor", tmpa3,
                  pb[6][:, 0:128].rearrange("p (q t) -> p q t", q=4),
                  ra3[:, :, cc0:cc0 + CH], ALU.mult)
            kb.vdrain()
            kb.op("vector", "tensor_tensor",
                  ys3[:, :, cc0:cc0 + CH], tmpa3,
                  pb[3][:, 0:128].rearrange("p (q t) -> p q t", q=4),
                  ALU.add, inc="sD")
            kb.vdrain()
            kb.op("vector", "tensor_tensor", tmpa3,
                  ys3[:, :, cc0:cc0 + CH], ys3[:, :, cc0:cc0 + CH],
                  ALU.mult, inc="sD")
            if DEV_P2STAGE == 35:
                continue
            kb.wait("tensor", "sD", kb.cnt["sD"])
            kb.op("tensor", "matmul", pb[2][0:4, 0:128], c["bd32"][:, :],
                  tmpa[:, 0:128], start=True, stop=True, inc="mm")
            kb.wait("vector", "mm", kb.cnt["mm"])
            kb.op("vector", "tensor_copy",
                  ssh[:, :].rearrange("p (q t) -> p q t", q=4)
                  [:, :, cc0:cc0 + CH],
                  pb[2][0:4, 0:128].rearrange("p (q t) -> p q t", q=4),
                  inc="sD")
            if DEV_P2STAGE == 36:
                continue
            for h in range(H):
                j, hl2 = h // 2, h % 2
                q, hl = h // 4, h % 4
                kb.op("tensor", "matmul",
                      pb[1][64 * hl2:64 * hl2 + 64, 32 * j:32 * j + 32],
                      Ktm[0:32, 128 * j + 64 * hl2:128 * j + 64 * hl2 + 64],
                      Utm[0:32, 128 * q + 32 * hl:128 * q + 32 * hl + 32],
                      start=True, stop=True,
                      tile_position=(0, 64 * hl2), skip_group_check=True)
            kb.mark("tensor", "mm")
            if DEV_P2STAGE == 37:
                continue
            kb.wait("scalar", "sB", kb.cnt["sB"])
            kb.wait("scalar", "vv", kb.cnt["vv"])
            for j in range(8):
                kb.op("scalar", "activation", rsy_rep[:, 32 * j:32 * j + 32],
                      S_all[:, 32 * j:32 * j + 32], AF.Copy,
                      scale=acg[:, NCH * j + li:NCH * j + li + 1], inc="sc")
            kb.wait("vector", "mm", kb.cnt["mm"])
            kb.wait("vector", "sc", kb.cnt["sc"])
            for j in range(8):
                kb.op("vector", "tensor_tensor",
                      S_all[:, 32 * j:32 * j + 32],
                      rsy_rep[:, 32 * j:32 * j + 32],
                      pb[1][:, 32 * j:32 * j + 32], ALU.add)
            kb.mark("vector", "sB")
        if DEV_P2STAGE == 3:
            continue

        kb.wait("gpsimd", "sD", kb.cnt["sD"])
        sshv = ssh[:, :].rearrange("p (q t) -> p q t", q=4)
        kb.half_dma("gpsimd",
                    g["ssX"][0, :, t0:t0 + RNG]
                    .rearrange("(hl q) t -> hl q t", hl=4),
                    g["ssX"][1, :, t0:t0 + RNG]
                    .rearrange("(hl q) t -> hl q t", hl=4),
                    sshv, sshv, inc="sg")
        kb.wait("sync", "sg", kb.cnt["sg"])
        kb.post_flag(r, r)
        kb.poll_flag(r, MAGIC + r)
        kb.wait("sync", "s2", kb.cnt["s2"])
        kb.half_dma("sync", ssxA[:, :], ssxA[:, :],
                    g["ssX"][1, :, t0:t0 + RNG], g["ssX"][0, :, t0:t0 + RNG],
                    inc="s2")
        kb.half_dma("sync", ssxB[:, :], ssxB[:, :],
                    g["ssX"][0, :, t0:t0 + RNG], g["ssX"][1, :, t0:t0 + RNG],
                    inc="s2")
        kb.wait("vector", "s2", kb.cnt["s2"])
        kb.op("vector", "tensor_tensor", ssxA[:, :], ssxA[:, :], ssxB[:, :],
              ALU.add)
        kb.vdrain()
        kb.op("vector", "tensor_scalar", ssxA[:, :], ssxA[:, :], 1.0 / DH,
              None, op0=ALU.mult, inc="vv")
        kb.wait("scalar", "vv", kb.cnt["vv"])
        kb.op("scalar", "activation", rstdy[:, :], ssxA[:, :], AF.Ln,
              bias=EPS_RMS, inc="sc")
        kb.wait("scalar", "sc", kb.cnt["sc"])
        kb.op("scalar", "activation", rstdy[:, :], rstdy[:, :], AF.Exp,
              scale=-0.5, inc="sc")
        kb.wait("gpsimd", "sc", kb.cnt["sc"])
        kb.wait("gpsimd", "s7", kb.cnt["s7"])
        kb.dma("gpsimd", rsF[0:1, :].rearrange("o (h t) -> (o h) t", h=H),
               rstdy[:, :], inc="s8")
        kb.wait("sync", "s8", kb.cnt["s8"])
        for qv in range(4):
            kb.dma("sync", rsy3[:, qv, :],
                   bass.AP(rsF.tensor, qv * RNG,
                           [[4 * RNG, 4], [0, EH], [1, RNG]]), inc="s7")
        kb.wait("vector", "s7", kb.cnt["s7"])
        kb.wait("vector", "sB", kb.cnt["sB"])
        kb.op("vector", "scalar_tensor_tensor", ys[:, :], ys[:, :],
              c["normw"][:, 0:1], rsy_rep[:, :], op0=ALU.mult, op1=ALU.mult)
        kb.op("vector", "tensor_tensor", ys[:, :], ys[:, :], Gs[:, :],
              ALU.mult)
        kb.op("vector", "tensor_copy", ctxhi[:, :], ys[:, :])
        kb.op("vector", "tensor_sub", ctxlo[:, :], ys[:, :], ctxhi[:, :],
              inc="vv")
        if "dbg_ys" in g and r == 0:
            kb.wait("gpsimd", "vv", kb.cnt["vv"])
            kb.dma("gpsimd", g["dbg_ys"][:, :], ys[:, :], inc="st")
        if "dbg_ctx" in g and r == 0:
            kb.wait("gpsimd", "vv", kb.cnt["vv"])
            kb.dma("gpsimd", g["dbg_ctx"][:, :], ys[:, :], inc="st")

        kb.wait("tensor", "vv", kb.cnt["vv"])
        ch3 = ctxhi[:, :].rearrange("p (q t) -> p q t", q=4)
        cl3 = ctxlo[:, :].rearrange("p (q t) -> p q t", q=4)
        for tc in range(RNG // 128):
            for oh in range(2):
                bank = pb[6 + oh]
                kb.wait("tensor", "sE", max(0, kb.cnt["sE"] - 1))
                for q in range(4):
                    kb.mm3(bank[:, :],
                           ch3[:, q, 128 * tc:128 * tc + 128],
                           cl3[:, q, 128 * tc:128 * tc + 128],
                           wo_hi[q][:, 512 * oh:512 * oh + 512],
                           wo_lo[q][:, 512 * oh:512 * oh + 512],
                           start=(q == 0), stop=(q == 3))
                kb.mark("tensor", "mm")
                kb.wait("vector", "mm", kb.cnt["mm"])
                kb.wait("vector", "sg", kb.cnt["sg"])
                kb.op("vector", "tensor_copy", woev[:, :], bank[:, :],
                      inc="sE")
                kb.wait("gpsimd", "sE", kb.cnt["sE"])
                kb.half_dma("gpsimd",
                            g["woP"][0, t0 + 128 * tc:t0 + 128 * tc + 128,
                                     512 * oh:512 * oh + 512],
                            g["woP"][1, t0 + 128 * tc:t0 + 128 * tc + 128,
                                     512 * oh:512 * oh + 512],
                            woev[:, :], woev[:, :], inc="sg")
        kb.wait("sync", "sg", kb.cnt["sg"])
        kb.post_flag(NRNG + r, NRNG + r)
    kb.barrier("ph2")


def phase3(kb):
    g, c = kb.g, kb.c
    kb.areset()
    mod = {k: kb.asb("m3_" + k, [128, D])
           for k in ["cb1", "sbv", "ga", "gabo", "n1w"]}
    rep_rows(kb, c["modrow"][0:1, 4 * D:5 * D], mod["cb1"])
    rep_rows(kb, c["modrow"][0:1, 3 * D:4 * D], mod["sbv"])
    rep_rows(kb, c["modrow"][0:1, 2 * D:3 * D], mod["ga"])
    rep_rows(kb, c["bo_row"][0:1, :], mod["gabo"])
    rep_rows(kb, c["n1w_row"][0:1, :], mod["n1w"])
    kb.op("vector", "tensor_tensor", mod["gabo"][:, :], mod["gabo"][:, :],
          mod["ga"][:, :], ALU.mult, inc="vv")
    xt = kb.asb("p3x", [128, D])
    pA = kb.asb("p3a", [128, D])
    pBt = kb.asb("p3b", [128, D])
    h2 = kb.asb("p3h2", [128, D])
    r1 = kb.asb("p3r1", [128, D])
    rhi = kb.asb("p3rhi", [128, D], BF16)
    rlo = kb.asb("p3rlo", [128, D], BF16)
    sq = kb.asb("p3sq", [128, D])
    cols = kb.asb("p3cols", [128, 8])
    SUM, SSQ, NEGM, VAR, RSTD = 0, 1, 2, 3, 4
    kb.poll_flag(NRNG + DEV_NRNG - 1, MAGIC + NRNG + DEV_NRNG - 1)
    kb.wait("sync", "sg", kb.cnt["sg"])
    ntile3 = (DEV_NRNG * RNG) // 128 if DEV_NRNG < NRNG else L // 128
    for i in range(ntile3):
        kb.wait("sync", "wl", max(0, kb.cnt["wl"] - 32))
        kb.dma("sync", xt[:], g["xb"][128 * i:128 * i + 128, :], inc="dl")
        kb.dma("sync", pA[:], g["woP"][0, 128 * i:128 * i + 128, :], inc="dl")
        kb.dma("sync", pBt[:], g["woP"][1, 128 * i:128 * i + 128, :],
               inc="dl")
        kb.wait("vector", "dl", kb.cnt["dl"])
        kb.op("vector", "tensor_add", pA[:], pA[:], pBt[:])
        kb.op("vector", "tensor_tensor", pA[:], pA[:], mod["ga"][:, :],
              ALU.mult)
        kb.op("vector", "tensor_add", pA[:], pA[:], mod["gabo"][:, :])
        kb.op("vector", "tensor_add", xt[:], xt[:], pA[:], inc="vv")
        kb.wait("gpsimd", "vv", kb.cnt["vv"])
        kb.dma("gpsimd", g["x2s"][128 * i:128 * i + 128, :], xt[:], inc="wl")
        if "dbg_x2" in g and i < 4:
            kb.dma("gpsimd", g["dbg_x2"][128 * i:128 * i + 128, :], xt[:],
                   inc="st")
        kb.wait("scalar", "vv", kb.cnt["vv"])
        kb.op("scalar", "activation", sq[:], xt[:], AF.Square,
              accum_out=cols[:, SSQ:SSQ + 1])
        kb.op("scalar", "activation", sq[:], xt[:], AF.Copy,
              accum_out=cols[:, SUM:SUM + 1], inc="sc")
        kb.wait("vector", "sc", kb.cnt["sc"])
        kb.op("vector", "tensor_scalar", cols[:, NEGM:NEGM + 1],
              cols[:, SUM:SUM + 1], -1.0 / D, None, op0=ALU.mult)
        kb.vdrain()
        kb.op("vector", "tensor_tensor", cols[:, VAR:VAR + 1],
              cols[:, NEGM:NEGM + 1], cols[:, NEGM:NEGM + 1], ALU.mult)
        kb.op("vector", "tensor_scalar", cols[:, 5:6],
              cols[:, SSQ:SSQ + 1], 1.0 / D, None, op0=ALU.mult)
        kb.vdrain()
        kb.op("vector", "tensor_tensor", cols[:, VAR:VAR + 1],
              cols[:, 5:6], cols[:, VAR:VAR + 1], ALU.subtract, inc="vv")
        kb.wait("scalar", "vv", kb.cnt["vv"])
        kb.op("scalar", "activation", cols[:, RSTD:RSTD + 1],
              cols[:, VAR:VAR + 1], AF.Ln, bias=EPS_LN, inc="sc")
        kb.wait("scalar", "sc", kb.cnt["sc"])
        kb.op("scalar", "activation", cols[:, RSTD:RSTD + 1],
              cols[:, RSTD:RSTD + 1], AF.Exp, scale=-0.5, inc="sc")
        kb.wait("vector", "sc", kb.cnt["sc"])
        kb.wait("vector", "wl", max(0, kb.cnt["wl"] - 32))
        kb.op("vector", "scalar_tensor_tensor", h2[:], xt[:],
              cols[:, NEGM:NEGM + 1], mod["cb1"][:, :], op0=ALU.add,
              op1=ALU.mult)
        kb.op("vector", "scalar_tensor_tensor", h2[:], h2[:],
              cols[:, RSTD:RSTD + 1], mod["sbv"][:, :], op0=ALU.mult,
              op1=ALU.add, inc="vv")
        kb.wait("gpsimd", "vv", kb.cnt["vv"])
        kb.dma("gpsimd", g["h2s"][128 * i:128 * i + 128, :], h2[:], inc="wl")
        kb.wait("scalar", "vv", kb.cnt["vv"])
        kb.op("scalar", "activation", sq[:], h2[:], AF.Square,
              accum_out=cols[:, SSQ:SSQ + 1], inc="sc")
        kb.wait("scalar", "sc", kb.cnt["sc"])
        kb.op("scalar", "activation", cols[:, VAR:VAR + 1],
              cols[:, SSQ:SSQ + 1], AF.Ln, scale=1.0 / D, bias=EPS_RMS,
              inc="sc")
        kb.wait("scalar", "sc", kb.cnt["sc"])
        kb.op("scalar", "activation", cols[:, RSTD:RSTD + 1],
              cols[:, VAR:VAR + 1], AF.Exp, scale=-0.5, inc="sc")
        kb.wait("vector", "sc", kb.cnt["sc"])
        kb.op("vector", "scalar_tensor_tensor", r1[:], h2[:],
              cols[:, RSTD:RSTD + 1], mod["n1w"][:, :], op0=ALU.mult,
              op1=ALU.mult)
        kb.wait("vector", "wl", kb.cnt["wl"])
        kb.op("vector", "tensor_copy", rhi[:], r1[:])
        kb.op("vector", "tensor_sub", rlo[:], r1[:], rhi[:], inc="vv")
        kb.wait("gpsimd", "vv", kb.cnt["vv"])
        kb.dma("gpsimd", g["r1hi"][128 * i:128 * i + 128, :], rhi[:],
               inc="wl")
        kb.dma("gpsimd", g["r1lo"][128 * i:128 * i + 128, :], rlo[:],
               inc="wl")
        if "dbg_r1" in g and i < 4:
            kb.dma("gpsimd", g["dbg_r1"][128 * i:128 * i + 128, :], r1[:],
                   inc="st")
    kb.barrier("ph3")


def phase4(kb):
    g, c, pb = kb.g, kb.c, kb.pb
    nc = kb.nc
    kb.areset()
    for nm in ["fqT", "fkT", "fvT"]:
        g[nm + "hi"] = nc.dram_tensor(nm + "hi", [D, NDIMS], BF16,
                                      kind="Internal").ap()
        g[nm + "lo"] = nc.dram_tensor(nm + "lo", [D, NDIMS], BF16,
                                      kind="Internal").ap()
    for nm in ["fqT", "fkT", "fvT"]:
        split_weight_to_scratch(kb, nm, NDIMS, nm + "hi", nm + "lo")
    kb.barrier("p4w")

    r1f_hi = [kb.asb("r1fh%d" % j, [128, RNG], BF16) for j in range(8)]
    r1f_lo = [kb.asb("r1fl%d" % j, [128, RNG], BF16) for j in range(8)]
    wbh = [kb.asb("p4wh%d" % i, [128, NDIMS], BF16) for i in range(2)]
    wbl = [kb.asb("p4wl%d" % i, [128, NDIMS], BF16) for i in range(2)]
    evh = kb.asb("p4evh", [128, RNG], BF16)
    evl = kb.asb("p4evl", [128, RNG], BF16)
    vst = kb.asb("p4vst", [128, FH * 65])

    kb.wait("sync", "wl", kb.cnt["wl"])
    kb.wait("scalar", "wl", kb.cnt["wl"])
    for r in range(DEV_NRNG):
        t0 = RNG * r
        kb.wait("scalar", "s0", kb.cnt["s0"])
        for j in range(8):
            kb.dma("scalar", r1f_hi[j][:, :],
                   g["r1hi"][t0:t0 + RNG, 128 * j:128 * j + 128],
                   transpose=True, inc="s5")
            kb.dma("scalar", r1f_lo[j][:, :],
                   g["r1lo"][t0:t0 + RNG, 128 * j:128 * j + 128],
                   transpose=True, inc="s5")
        kb.wait("tensor", "s5", kb.cnt["s5"])
        for nm, hiname, loname in [("fkT", "Kfhi", "Kflo"),
                                   ("fqT", "Qfhi", "Qflo")]:
            for j in range(8):
                bi = j % 2
                kb.wait("sync", "s0", max(0, kb.cnt["s0"] - 1))
                kb.dma("sync", wbh[bi][:, :],
                       g[nm + "hi"][128 * j:128 * j + 128, :], inc="dl")
                kb.dma("sync", wbl[bi][:, :],
                       g[nm + "lo"][128 * j:128 * j + 128, :], inc="dl")
                if j == 0:
                    kb.wait("tensor", "vv", kb.cnt["vv"])
                kb.wait("tensor", "dl", kb.cnt["dl"])
                for oc in range(4):
                    kb.mm3(pb[oc][:, 0:RNG],
                           wbh[bi][:, 128 * oc:128 * oc + 128],
                           wbl[bi][:, 128 * oc:128 * oc + 128],
                           r1f_hi[j][:, :], r1f_lo[j][:, :],
                           start=(j == 0), stop=(j == 7))
                kb.mark("tensor", "s0")
            for oc in range(4):
                kb.wait("vector", "s0", kb.cnt["s0"])
                kb.wait("vector", "wl", kb.cnt["wl"])
                kb.op("vector", "tensor_copy", evh[:, :], pb[oc][:, 0:RNG])
                kb.op("vector", "tensor_sub", evl[:, :], pb[oc][:, 0:RNG],
                      evh[:, :], inc="vv")
                kb.wait("gpsimd", "vv", kb.cnt["vv"])
                kb.dma("gpsimd",
                       g[hiname][128 * oc:128 * oc + 128, t0:t0 + RNG],
                       evh[:, :], inc="wl")
                kb.dma("gpsimd",
                       g[loname][128 * oc:128 * oc + 128, t0:t0 + RNG],
                       evl[:, :], inc="wl")
        for tc in range(RNG // 128):
            for j in range(8):
                bi = j % 2
                kb.wait("sync", "s0", max(0, kb.cnt["s0"] - 1))
                kb.dma("sync", wbh[bi][:, :],
                       g["fvThi"][128 * j:128 * j + 128, :], inc="dl")
                kb.dma("sync", wbl[bi][:, :],
                       g["fvTlo"][128 * j:128 * j + 128, :], inc="dl")
                if j == 0:
                    kb.wait("tensor", "vv", kb.cnt["vv"])
                kb.wait("tensor", "dl", kb.cnt["dl"])
                kb.mm3(pb[4][:, 0:NDIMS],
                       r1f_hi[j][:, 128 * tc:128 * tc + 128],
                       r1f_lo[j][:, 128 * tc:128 * tc + 128],
                       wbh[bi][:, :], wbl[bi][:, :],
                       start=(j == 0), stop=(j == 7))
                kb.mark("tensor", "s0")
            kb.wait("vector", "s0", kb.cnt["s0"])
            kb.wait("vector", "wl", kb.cnt["wl"])
            kb.op("vector", "tensor_copy",
                  vst[:, :].rearrange("p (h e) -> p h e", h=FH)[:, :, 0:64],
                  pb[4][:, 0:NDIMS].rearrange("p (h e) -> p h e", h=FH))
            kb.op("vector", "memset",
                  vst[:, :].rearrange("p (h e) -> p h e", h=FH)[:, :, 64:65],
                  1.0, inc="vv")
            kb.wait("gpsimd", "vv", kb.cnt["vv"])
            kb.dma("gpsimd", g["Vaug"][t0 + 128 * tc:t0 + 128 * tc + 128, :],
                   vst[:, :], inc="wl")
    kb.barrier("ph4")


def phase5(kb):
    QBS = 512
    g, c, pb = kb.g, kb.c, kb.pb
    kb.areset()
    ow_hi = [kb.asb("ow_hi%d" % h, [64, D], BF16) for h in range(8)]
    ow_lo = [kb.asb("ow_lo%d" % h, [64, D], BF16) for h in range(8)]
    stg = kb.asb("p5stg", [64, D])
    for h in range(8):
        kb.wait("sync", "vv", kb.cnt["vv"])
        kb.dma("sync", stg[:, :], g["owT"][64 * h:64 * h + 64, :], inc="dl")
        kb.wait("vector", "dl", kb.cnt["dl"])
        kb.op("vector", "tensor_copy", ow_hi[h][:, :], stg[:, :])
        kb.op("vector", "tensor_sub", ow_lo[h][:, :], stg[:, :],
              ow_hi[h][:, :], inc="vv")

    qbh = [kb.asb("p5qh%d" % i, [128, QBS], BF16) for i in range(4)]
    qbl = [kb.asb("p5ql%d" % i, [128, QBS], BF16) for i in range(4)]
    kth = [kb.asb("p5kh%d" % i, [128, 128], BF16) for i in range(2)]
    ktl = [kb.asb("p5kl%d" % i, [128, 128], BF16) for i in range(2)]
    vtt = kb.asb("p5vt", [128, FH * 65])
    Pt = [kb.asb("p5P%d" % i, [128, QBS]) for i in range(2)]
    msk = kb.asb("p5msk", [128, QBS])
    rz = kb.asb("p5rz", [1, QBS])
    af_h = [kb.asb("p5afh%d" % h, [64, QBS], BF16) for h in range(8)]
    af_l = [kb.asb("p5afl%d" % h, [64, QBS], BF16) for h in range(8)]
    aft = kb.asb("p5aft", [64, QBS])
    owev = kb.asb("p5owev", [128, QBS])

    kb.wait("scalar", "wl", kb.cnt["wl"])
    avail = DEV_NRNG * RNG
    for qi in range(8):
        q0_0 = 512 * (2 * qi)
        q0_1 = 512 * (2 * qi + 1)
        NKT = 8 * qi + 8
        if 512 * (2 * qi + 2) > avail or 128 * NKT > avail:
            continue
        kb.wait("scalar", "sE", kb.cnt["sE"])
        for dt_ in range(4):
            kb.half_dma("scalar", qbh[dt_][:, :], qbh[dt_][:, :],
                        g["Qfhi"][128 * dt_:128 * dt_ + 128, q0_0:q0_0 + 512],
                        g["Qfhi"][128 * dt_:128 * dt_ + 128, q0_1:q0_1 + 512],
                        inc="s5", key=1)
            kb.half_dma("scalar", qbl[dt_][:, :], qbl[dt_][:, :],
                        g["Qflo"][128 * dt_:128 * dt_ + 128, q0_0:q0_0 + 512],
                        g["Qflo"][128 * dt_:128 * dt_ + 128, q0_1:q0_1 + 512],
                        inc="s5", key=1)
        for hp in range(2):
            kb.wait("tensor", "sE", kb.cnt["sE"])
            kb.wait("tensor", "vv", kb.cnt["vv"])
            for kt in range(NKT):
                kb.wait("scalar", "s6", max(0, kb.cnt["s6"] - 4))
                for hh in range(2):
                    hpair = 2 * hp + hh
                    kb.dma("scalar", kth[hh][:, :],
                           g["Kfhi"][128 * hpair:128 * hpair + 128,
                                     128 * kt:128 * kt + 128], inc="s5")
                    kb.dma("scalar", ktl[hh][:, :],
                           g["Kflo"][128 * hpair:128 * hpair + 128,
                                     128 * kt:128 * kt + 128], inc="s5")
                kb.dma("scalar", vtt[:, :],
                       g["Vaug"][128 * kt:128 * kt + 128, :], inc="s5")
                kb.wait("tensor", "s5", kb.cnt["s5"])
                for hh in range(2):
                    for hl2 in range(2):
                        h4 = 2 * hh + hl2
                        kb.mm3(pb[h4][:, 0:QBS],
                               kth[hh][64 * hl2:64 * hl2 + 64, :],
                               ktl[hh][64 * hl2:64 * hl2 + 64, :],
                               qbh[2 * hp + hh][64 * hl2:64 * hl2 + 64, :],
                               qbl[2 * hp + hh][64 * hl2:64 * hl2 + 64, :],
                               start=True, stop=True,
                               tile_position=(64 * hl2, 0))
                kb.mark("tensor", "mm")
                masked = kt >= 8 * qi
                if masked:
                    mi = kt - 8 * qi
                    kb.wait("vector", "mm", kb.cnt["mm"])
                    kb.op("vector", "tensor_scalar", msk[:, :],
                          c["iotaQ"][:, :],
                          c["qoffc"][:, 8 * qi + mi:8 * qi + mi + 1], None,
                          op0=ALU.add)
                    kb.op("vector", "tensor_scalar", msk[:, :],
                          msk[:, :], 0.0, None, op0=ALU.is_ge, inc="vv")
                for h4 in range(4):
                    habs = 4 * hp + h4
                    kb.wait("scalar", "mm", kb.cnt["mm"])
                    kb.wait("scalar", "s6", kb.cnt["s6"])
                    kb.op("scalar", "activation", Pt[h4 % 2][:, :],
                          pb[h4][:, 0:QBS], AF.Exp, scale=0.125, inc="sc")
                    if masked:
                        kb.wait("vector", "sc", kb.cnt["sc"])
                        kb.wait("vector", "vv", kb.cnt["vv"])
                        kb.op("vector", "tensor_tensor", Pt[h4 % 2][:, :],
                              Pt[h4 % 2][:, :], msk[:, :], ALU.mult,
                              inc="sc")
                    kb.wait("tensor", "sc", kb.cnt["sc"])
                    kb.op("tensor", "matmul", pb[4 + h4][0:65, 0:QBS],
                          vtt[:, 65 * habs:65 * habs + 65], Pt[h4 % 2][:, :],
                          start=(kt == 0), stop=(kt == NKT - 1), inc="s6")
            for h4 in range(4):
                habs = 4 * hp + h4
                kb.wait("vector", "s6", kb.cnt["s6"])
                kb.op("vector", "reciprocal", rz[0:1, :],
                      pb[4 + h4][64:65, 0:QBS], inc="vv")
                kb.wait("tensor", "vv", kb.cnt["vv"])
                kb.op("tensor", "matmul", pb[h4][0:64, 0:QBS],
                      c["ones_row"][0:1, 0:64], rz[0:1, :],
                      start=True, stop=True, inc="mm")
                kb.wait("vector", "mm", kb.cnt["mm"])
                kb.op("vector", "tensor_copy", aft[:, :],
                      pb[4 + h4][0:64, 0:QBS])
                kb.op("vector", "tensor_tensor", aft[:, :],
                      aft[:, :], pb[h4][0:64, 0:QBS], ALU.mult)
                kb.op("vector", "tensor_copy", af_h[habs][:, :], aft[:, :])
                kb.op("vector", "tensor_sub", af_l[habs][:, :], aft[:, :],
                      af_h[habs][:, :], inc="sE")
        kb.wait("tensor", "sE", kb.cnt["sE"])
        for tc in range(4):
            for oh in range(2):
                bank = pb[6 + (tc % 2)]
                kb.wait("tensor", "vv", kb.cnt["vv"])
                for h in range(8):
                    kb.mm3(bank[:, :],
                           af_h[h][:, 128 * tc:128 * tc + 128],
                           af_l[h][:, 128 * tc:128 * tc + 128],
                           ow_hi[h][:, 512 * oh:512 * oh + 512],
                           ow_lo[h][:, 512 * oh:512 * oh + 512],
                           start=(h == 0), stop=(h == 7))
                kb.mark("tensor", "mm")
                kb.wait("vector", "mm", kb.cnt["mm"])
                kb.wait("vector", "sg", kb.cnt["sg"])
                kb.op("vector", "tensor_copy", owev[:, :], bank[:, :],
                      inc="vv")
                kb.wait("gpsimd", "vv", kb.cnt["vv"])
                kb.quad_dma("gpsimd",
                            [g["owP"][sv,
                                      512 * (2 * qi + pv) + 128 * tc:
                                      512 * (2 * qi + pv) + 128 * tc + 128,
                                      512 * oh:512 * oh + 512]
                             for pv in range(2) for sv in range(2)],
                            [owev[:, :]] * 4, inc="sg")
        kb.wait("sync", "sg", kb.cnt["sg"])
        kb.post_flag(2 * NRNG + qi, 2 * NRNG + qi)
        if "dbg_attn" in g and qi == 0:
            kb.wait("gpsimd", "sE", kb.cnt["sE"])
            kb.dma("gpsimd", g["dbg_attn"][0:64, :], aft[:, :], inc="st")
    kb.barrier("ph5")


def phase6(kb):
    g, c, pb = kb.g, kb.c, kb.pb
    nc = kb.nc
    kb.areset()
    mod = {"gb": kb.asb("m6_gb", [128, D]), "n2w": kb.asb("m6_n2w", [128, D])}
    rep_rows(kb, c["modrow"][0:1, 5 * D:6 * D], mod["gb"])
    rep_rows(kb, c["n2w_row"][0:1, :], mod["n2w"])
    for nm, rows, w in [("gateT", D, 8), ("upT", D, ED)]:
        g[nm + "hi"] = nc.dram_tensor(nm + "hi", [rows, w], BF16,
                                      kind="Internal").ap()
        g[nm + "lo"] = nc.dram_tensor(nm + "lo", [rows, w], BF16,
                                      kind="Internal").ap()
    g["downThi"] = nc.dram_tensor("downThi", [ED, D], BF16,
                                  kind="Internal").ap()
    g["downTlo"] = nc.dram_tensor("downTlo", [ED, D], BF16,
                                  kind="Internal").ap()
    stg4 = kb.asb("p6stg", [128, D])
    shi4 = kb.asb("p6shi", [128, D], BF16)
    slo4 = kb.asb("p6slo", [128, D], BF16)
    chunks = ([("gateT", j, 0, 8) for j in range(8)]
              + [("upT", j, cc, D) for j in range(8) for cc in range(4)]
              + [("downT", j, 0, D) for j in range(ED // 128)])
    for nm, j, cc, w in chunks:
        kb.wait("sync", "vv", kb.cnt["vv"])
        kb.dma("sync", stg4[:, 0:w], g[nm][128 * j:128 * j + 128,
                                           cc * D:cc * D + w], inc="dl")
        kb.wait("vector", "dl", kb.cnt["dl"])
        kb.wait("vector", "wl", kb.cnt["wl"])
        kb.op("vector", "tensor_copy", shi4[:, 0:w], stg4[:, 0:w])
        kb.vdrain()
        kb.op("vector", "tensor_sub", slo4[:, 0:w], stg4[:, 0:w],
              shi4[:, 0:w], inc="vv")
        kb.wait("gpsimd", "vv", kb.cnt["vv"])
        kb.dma("gpsimd", g[nm + "hi"][128 * j:128 * j + 128, cc * D:cc * D + w],
               shi4[:, 0:w], inc="wl")
        kb.dma("gpsimd", g[nm + "lo"][128 * j:128 * j + 128, cc * D:cc * D + w],
               slo4[:, 0:w], inc="wl")
    kb.barrier("p6w")

    kb.poll_flag(2 * NRNG + 7, MAGIC + 2 * NRNG + 7)

    xf = kb.asb("p6xf", [128, D])
    ow2 = kb.asb("p6ow2", [128, D])
    r2 = kb.asb("p6r2", [128, D])
    r2h = kb.asb("p6r2h", [128, D], BF16)
    r2l = kb.asb("p6r2l", [128, D], BF16)
    sq = kb.asb("p6sq", [128, D])
    cols = kb.asb("p6cols", [128, 16])
    SSQ, VAR, RSTD = 1, 3, 4
    def quad_dma(eng, outs, ins, inc=None):
        cc = kb.c
        sems = kb.sems
        if inc is not None:
            kb.cnt[inc] += 16

        def f(h, outs=outs, ins=ins, inc=inc):
            KB._rc[0] += 1
            with h.register("hr%d" % KB._rc[0]) as hr, h.register("pr%d" % KB._rc[0]) as prr:
                h.reg_load(hr, cc["meta"][0:1, 0:1])
                h.reg_load(prr, cc["meta"][0:1, 1:2])
                for sv in range(2):
                    for pv in range(2):
                        idx = sv + 2 * pv
                        with h.If_eq(hr, sv):
                            with h.If_eq(prr, pv):
                                ins_i = h.dma_start(out=outs[idx],
                                                    in_=ins[idx])
                                if inc is not None:
                                    ins_i.then_inc(sems[inc], 16)
        kb.q[eng].append(f)

    def myrow(t, sv, pv):
        return 512 * (2 * (2 * t + sv) + pv)

    for t in range(4):
        for tc in range(4):
            outs_h2 = [xf[:] for _ in range(4)]
            ins_h2 = [g["h2s"][myrow(t, sv, pv) + 128 * tc:
                              myrow(t, sv, pv) + 128 * tc + 128, :]
                      for pv in range(2) for sv in range(2)]
            ins_h2 = [g["h2s"][myrow(t, idx % 2, idx // 2) + 128 * tc:
                              myrow(t, idx % 2, idx // 2) + 128 * tc + 128, :]
                      for idx in range(4)]
            ins_owA = [g["owP"][0][myrow(t, idx % 2, idx // 2) + 128 * tc:
                                   myrow(t, idx % 2, idx // 2) + 128 * tc + 128, :]
                       for idx in range(4)]
            ins_owB = [g["owP"][1][myrow(t, idx % 2, idx // 2) + 128 * tc:
                                   myrow(t, idx % 2, idx // 2) + 128 * tc + 128, :]
                       for idx in range(4)]
            kb.wait("sync", "vv", kb.cnt["vv"])
            kb.wait("sync", "wl", kb.cnt["wl"])
            kb.wait("sync", "sg", kb.cnt["sg"])
            quad_dma("sync", outs_h2, ins_h2, inc="dl")
            quad_dma("sync", [ow2[:]] * 4, ins_owA, inc="dl")
            kb.wait("vector", "dl", kb.cnt["dl"])
            kb.op("vector", "tensor_add", xf[:], xf[:], ow2[:], inc="vv")
            quad_dma("sync", [ow2[:]] * 4, ins_owB, inc="dl")
            kb.wait("vector", "dl", kb.cnt["dl"])
            kb.op("vector", "tensor_add", xf[:], xf[:], ow2[:], inc="vv")
            i128 = 4 * t + tc
            kb.wait("gpsimd", "vv", kb.cnt["vv"])
            kb.dma("gpsimd", g["xf1_s"][128 * i128:128 * i128 + 128, :],
                   xf[:], inc="wl")
            kb.wait("scalar", "vv", kb.cnt["vv"])
            kb.op("scalar", "activation", sq[:], xf[:], AF.Square,
                  accum_out=cols[:, SSQ:SSQ + 1], inc="sc")
            kb.wait("scalar", "sc", kb.cnt["sc"])
            kb.op("scalar", "activation", cols[:, VAR:VAR + 1],
                  cols[:, SSQ:SSQ + 1], AF.Ln, scale=1.0 / D, bias=EPS_RMS,
                  inc="sc")
            kb.wait("scalar", "sc", kb.cnt["sc"])
            kb.op("scalar", "activation", cols[:, RSTD:RSTD + 1],
                  cols[:, VAR:VAR + 1], AF.Exp, scale=-0.5, inc="sc")
            kb.wait("vector", "sc", kb.cnt["sc"])
            kb.op("vector", "scalar_tensor_tensor", r2[:], xf[:],
                  cols[:, RSTD:RSTD + 1], mod["n2w"][:, :], op0=ALU.mult,
                  op1=ALU.mult)
            kb.wait("vector", "wl", kb.cnt["wl"])
            kb.op("vector", "tensor_copy", r2h[:], r2[:])
            kb.op("vector", "tensor_sub", r2l[:], r2[:], r2h[:], inc="vv")
            kb.wait("gpsimd", "vv", kb.cnt["vv"])
            kb.dma("gpsimd", g["r2hi_s"][128 * i128:128 * i128 + 128, :],
                   r2h[:], inc="wl")
            kb.dma("gpsimd", g["r2lo_s"][128 * i128:128 * i128 + 128, :],
                   r2l[:], inc="wl")
            if "dbg_xf1" in g and i128 < 4:
                kb.dma("gpsimd", g["dbg_xf1"][128 * i128:128 * i128 + 128, :],
                       xf[:], inc="st")
    kb.barrier("p6a")

    r2f_h = [kb.asb("r2fh%d" % j, [128, 512], BF16) for j in range(8)]
    r2f_l = [kb.asb("r2fl%d" % j, [128, 512], BF16) for j in range(8)]
    wgh = [kb.asb("p6wgh%d" % i, [128, 8], BF16) for i in range(2)]
    wgl = [kb.asb("p6wgl%d" % i, [128, 8], BF16) for i in range(2)]
    gl = kb.asb("p6gl", [128, 8])
    ge = kb.asb("p6ge", [128, 8])
    gcols = kb.asb("p6gcols", [128, 8])
    factor = kb.asb("p6factor", [128, 16])
    MX, Z8, M1, M2, S12 = 0, 1, 2, 3, 4
    uph = [kb.asb("p6uph%d" % i, [128, 128], BF16) for i in range(2)]
    upl = [kb.asb("p6upl%d" % i, [128, 128], BF16) for i in range(2)]
    uev = kb.asb("p6uev", [128, 512])
    uevh = kb.asb("p6uevh", [128, 512], BF16)
    uevl = kb.asb("p6uevl", [128, 512], BF16)
    for tp in range(4):
        kb.wait("scalar", "s0", kb.cnt["s0"])
        kb.wait("scalar", "wl", kb.cnt["wl"])
        kb.wait("sync", "wl", kb.cnt["wl"])
        for j in range(8):
            kb.dma("scalar", r2f_h[j][:, :],
                   g["r2hi_s"][512 * tp:512 * tp + 512,
                               128 * j:128 * j + 128], transpose=True,
                   inc="s5")
            kb.dma("scalar", r2f_l[j][:, :],
                   g["r2lo_s"][512 * tp:512 * tp + 512,
                               128 * j:128 * j + 128], transpose=True,
                   inc="s5")
        kb.wait("tensor", "s5", kb.cnt["s5"])
        for ti in range(4):
            tt = 4 * tp + ti
            kb.wait("tensor", "vv", kb.cnt["vv"])
            for j in range(8):
                bi = j % 2
                kb.wait("sync", "s0", max(0, kb.cnt["s0"] - 1))
                kb.dma("sync", wgh[bi][:, :],
                       g["gateThi"][128 * j:128 * j + 128, :], inc="dl")
                kb.dma("sync", wgl[bi][:, :],
                       g["gateTlo"][128 * j:128 * j + 128, :], inc="dl")
                kb.wait("tensor", "dl", kb.cnt["dl"])
                kb.mm3(pb[5][:, 0:8],
                       r2f_h[j][:, 128 * ti:128 * ti + 128],
                       r2f_l[j][:, 128 * ti:128 * ti + 128],
                       wgh[bi][:, :], wgl[bi][:, :],
                       start=(j == 0), stop=(j == 7))
                kb.mark("tensor", "s0")
            kb.wait("vector", "s0", kb.cnt["s0"])
            kb.op("vector", "tensor_copy", gl[:, :], pb[5][:, 0:8])
            kb.vdrain()
            kb.op("vector", "reduce_max", gcols[:, MX:MX + 1], gl[:, :],
                  mybir.AxisListType.X, negate=True, inc="vv")
            kb.wait("scalar", "vv", kb.cnt["vv"])
            kb.op("scalar", "activation", ge[:, :], gl[:, :], AF.Exp,
                  bias=gcols[:, MX:MX + 1], inc="sc")
            kb.wait("vector", "sc", kb.cnt["sc"])
            kb.op("vector", "reduce_sum", gcols[:, Z8:Z8 + 1], ge[:, :],
                  mybir.AxisListType.X)
            kb.op("vector", "reduce_max", gcols[:, M1:M1 + 1], ge[:, :],
                  mybir.AxisListType.X)
            kb.vdrain()
            kb.op("vector", "tensor_scalar", gl[:, :], ge[:, :],
                  gcols[:, M1:M1 + 1], None, op0=ALU.is_ge)
            kb.vdrain()
            kb.op("vector", "tensor_tensor", gl[:, :], gl[:, :], ge[:, :],
                  ALU.mult)
            kb.vdrain()
            kb.op("vector", "tensor_sub", gl[:, :], ge[:, :], gl[:, :])
            kb.vdrain()
            kb.op("vector", "reduce_max", gcols[:, M2:M2 + 1], gl[:, :],
                  mybir.AxisListType.X)
            kb.vdrain()
            kb.op("vector", "tensor_add", gcols[:, S12:S12 + 1],
                  gcols[:, M1:M1 + 1], gcols[:, M2:M2 + 1])
            kb.op("vector", "reciprocal", gcols[:, 5:6], gcols[:, Z8:Z8 + 1])
            kb.vdrain()
            kb.op("vector", "tensor_tensor", gcols[:, 6:7],
                  gcols[:, S12:S12 + 1], gcols[:, 5:6], ALU.mult)
            kb.vdrain()
            kb.op("vector", "tensor_scalar", gcols[:, 7:8],
                  gcols[:, 6:7], 1e-9, None, op0=ALU.add)
            kb.vdrain()
            kb.op("vector", "reciprocal", gcols[:, 7:8], gcols[:, 7:8])
            kb.vdrain()
            kb.op("vector", "tensor_tensor", factor[:, tt:tt + 1],
                  gcols[:, 6:7], gcols[:, 7:8], ALU.mult, inc="vv")
        for uc in range(ED // 128):
            kb.wait("tensor", "sc", kb.cnt["sc"])
            for j in range(8):
                bi = j % 2
                kb.wait("sync", "s0", max(0, kb.cnt["s0"] - 1))
                kb.dma("sync", uph[bi][:, :],
                       g["upThi"][128 * j:128 * j + 128,
                                  128 * uc:128 * uc + 128], inc="dl")
                kb.dma("sync", upl[bi][:, :],
                       g["upTlo"][128 * j:128 * j + 128,
                                  128 * uc:128 * uc + 128], inc="dl")
                kb.wait("tensor", "dl", kb.cnt["dl"])
                kb.mm3(pb[tp % 2][:, :], uph[bi][:, :], upl[bi][:, :],
                       r2f_h[j][:, :], r2f_l[j][:, :],
                       start=(j == 0), stop=(j == 7))
                kb.mark("tensor", "s0")
            kb.wait("scalar", "s0", kb.cnt["s0"])
            kb.op("scalar", "activation", uev[:, :], pb[tp % 2][:, :],
                  AF.Silu, inc="sc")
            kb.wait("vector", "sc", kb.cnt["sc"])
            kb.wait("vector", "wl", kb.cnt["wl"])
            kb.op("vector", "tensor_copy", uevh[:, :], uev[:, :])
            kb.op("vector", "tensor_sub", uevl[:, :], uev[:, :], uevh[:, :],
                  inc="vv")
            kb.wait("gpsimd", "vv", kb.cnt["vv"])
            kb.dma("gpsimd",
                   g["uphi_s"][128 * uc:128 * uc + 128,
                               512 * tp:512 * tp + 512], uevh[:, :],
                   inc="wl")
            kb.dma("gpsimd",
                   g["uplo_s"][128 * uc:128 * uc + 128,
                               512 * tp:512 * tp + 512], uevl[:, :],
                   inc="wl")
    kb.barrier("p6b")

    dwh = [kb.asb("p6dwh%d" % i, [128, D], BF16) for i in range(8)]
    dwl = [kb.asb("p6dwl%d" % i, [128, D], BF16) for i in range(8)]
    uph_t = [kb.asb("p6upt%d" % i, [128, 128], BF16) for i in range(8)]
    upl_t = [kb.asb("p6upt2%d" % i, [128, 128], BF16) for i in range(8)]
    acc_sb = kb.asb("p6acc", [128, D])
    xacc = kb.asb("p6xacc", [128, D])
    x2r = kb.asb("p6x2r", [128, D])
    outt = kb.asb("p6out", [128, D])
    dstg = kb.asb("p6dstg", [128, D])
    kb.wait("scalar", "wl", kb.cnt["wl"])
    kb.wait("sync", "wl", kb.cnt["wl"])
    for tt2 in range(16):
        for ecg in range(4):
            kb.wait("tensor", "vv", kb.cnt["vv"])
            for e8 in range(8):
                ec = 8 * ecg + e8
                kb.wait("sync", "s0", max(0, kb.cnt["s0"] - 2))
                kb.dma("sync", dwh[e8][:, :],
                       g["downThi"][128 * ec:128 * ec + 128, :], inc="dl")
                kb.dma("sync", dwl[e8][:, :],
                       g["downTlo"][128 * ec:128 * ec + 128, :], inc="dl")
                kb.wait("scalar", "s5", max(0, kb.cnt["s5"] - 32))
                kb.wait("scalar", "s0", max(0, kb.cnt["s0"] - 2))
                kb.dma("scalar", uph_t[e8][:, 0:128],
                       g["uphi_s"][128 * ec:128 * ec + 128,
                                   128 * tt2:128 * tt2 + 128], inc="s5")
                kb.dma("scalar", upl_t[e8][:, 0:128],
                       g["uplo_s"][128 * ec:128 * ec + 128,
                                   128 * tt2:128 * tt2 + 128], inc="s5")
                kb.wait("tensor", "dl", kb.cnt["dl"])
                kb.wait("tensor", "s5", kb.cnt["s5"])
                for oh in range(2):
                    kb.mm3(pb[oh][:, :],
                           uph_t[e8][:, 0:128], upl_t[e8][:, 0:128],
                           dwh[e8][:, 512 * oh:512 * oh + 512],
                           dwl[e8][:, 512 * oh:512 * oh + 512],
                           start=(e8 == 0), stop=(e8 == 7))
                kb.mark("tensor", "s0")
            kb.wait("vector", "s0", kb.cnt["s0"])
            if ecg == 0:
                kb.op("vector", "tensor_copy", acc_sb[:, 0:512], pb[0][:, :])
                kb.op("vector", "tensor_copy", acc_sb[:, 512:1024],
                      pb[1][:, :], inc="vv")
            else:
                kb.op("vector", "tensor_tensor", acc_sb[:, 0:512],
                      acc_sb[:, 0:512], pb[0][:, :], ALU.add)
                kb.op("vector", "tensor_tensor", acc_sb[:, 512:1024],
                      acc_sb[:, 512:1024], pb[1][:, :], ALU.add, inc="vv")
        ins_x2 = [g["x2s"][myrow(tt2 // 4, idx % 2, idx // 2)
                           + 128 * (tt2 % 4):
                           myrow(tt2 // 4, idx % 2, idx // 2)
                           + 128 * (tt2 % 4) + 128, :] for idx in range(4)]
        kb.wait("sync", "vv", kb.cnt["vv"])
        quad_dma("sync", [x2r[:]] * 4, ins_x2, inc="dl")
        kb.dma("sync", xacc[:], g["xf1_s"][128 * tt2:128 * tt2 + 128, :],
               inc="dl")
        kb.wait("vector", "dl", kb.cnt["dl"])
        kb.op("vector", "scalar_tensor_tensor", outt[:], acc_sb[:, :],
              factor[:, tt2:tt2 + 1], xacc[:], op0=ALU.mult, op1=ALU.add)
        kb.op("vector", "tensor_tensor", outt[:], outt[:], mod["gb"][:, :],
              ALU.mult)
        kb.op("vector", "tensor_add", outt[:], outt[:], x2r[:], inc="vv")
        kb.wait("gpsimd", "vv", kb.cnt["vv"])
        kb.dma("gpsimd", g["out_rows"][128 * tt2:128 * tt2 + 128, :],
               outt[:], inc="fin")
    kb.wait("gpsimd", "fin", kb.cnt["fin"])
    kb.barrier("done")


def kernel(**inputs):
    nc, g = build_program()
    in_maps = [build_inputs_for_core(cc, inputs) for cc in range(8)]
    res = run_bass_kernel_spmd(nc, in_maps, core_ids=list(range(8)))
    out = np.zeros((B, L, D), np.float32)
    for cc in range(8):
        b, P, s = cc // 4, (cc % 4) // 2, cc % 2
        rows = res.results[cc]["out_rows"]
        blocks = [j for j in range(16) if j % 2 == P and (j // 2) % 2 == s]
        for i, j in enumerate(blocks):
            out[b, 512 * j:512 * j + 512, :] = rows[512 * i:512 * i + 512, :]
    return out
